# version 14
# baseline (speedup 1.0000x reference)
import numpy as np
import ml_dtypes
import concourse.bass as bass
import concourse.mybir as mybir
from concourse.bass_utils import run_bass_kernel_spmd
from contextlib import ExitStack

F32 = mybir.dt.float32
BF16 = mybir.dt.bfloat16
I32 = mybir.dt.int32
AF = mybir.ActivationFunctionType
ALU = mybir.AluOpType
f32 = np.float32

B, S, D = 2, 8192, 4096
H, DH = 32, 128
DFF = 11008
NCORES = 8
OWN = 2048
HALO = OWN + 512
T = 512
NT = OWN // T
NCH = D // 128
FFCH = DFF // 128
ATT_SCALE = 1.0 / float(np.sqrt(DH))
EPS = 1e-5
NSLAB = 6
NXG = 4

ENGINES = ("sync", "tensor", "scalar", "vector", "gpsimd")
SEMS = (["ws%d" % i for i in range(NSLAB)]
        + ["xg%d" % i for i in range(NXG)]
        + ["xs0", "xs1", "sg0", "sg1",
           "os0", "os1", "ad", "od", "pe", "ac", "dv", "gp", "wb"])


def build_nc(ntiles=NT, debug=False, lockstep=False):
    nc = bass.Bass("TRN2", target_bir_lowering=False)

    xp1 = nc.dram_tensor("xp1", [NT * NCH * 128, 1024], F32,
                         kind="ExternalInput").ap()
    xp7 = nc.dram_tensor("xp7", [NT * NCH * 128, 512], F32,
                         kind="ExternalInput").ap()
    wqb = nc.dram_tensor("wqb", [8 * NCH * 128, 512], F32,
                         kind="ExternalInput").ap()
    wob = nc.dram_tensor("wob", [8 * NCH * 128, 512], F32,
                         kind="ExternalInput").ap()
    wkT = nc.dram_tensor("wkT", [D, DH], F32, kind="ExternalInput").ap()
    wvT = nc.dram_tensor("wvT", [D, DH], F32, kind="ExternalInput").ap()
    gb = nc.dram_tensor("gb", [22 * NCH * 128, 512], I32,
                        kind="ExternalInput").ap()
    ub = nc.dram_tensor("ub", [22 * NCH * 128, 512], I32,
                        kind="ExternalInput").ap()
    db = nc.dram_tensor("db", [8 * FFCH * 128, 512], I32,
                        kind="ExternalInput").ap()
    gsc_d = nc.dram_tensor("gsc", [128, FFCH], F32, kind="ExternalInput").ap()
    usc_d = nc.dram_tensor("usc", [128, FFCH], F32, kind="ExternalInput").ap()
    dsc_d = nc.dram_tensor("dsc", [128, NCH], F32, kind="ExternalInput").ap()
    qcos_d = nc.dram_tensor("qcosT", [DH, OWN], BF16, kind="ExternalInput").ap()
    qsin_d = nc.dram_tensor("qsinT", [DH, OWN], BF16, kind="ExternalInput").ap()
    kcos_d = nc.dram_tensor("kcosT", [DH, HALO], BF16, kind="ExternalInput").ap()
    ksin_d = nc.dram_tensor("ksinT", [DH, HALO], BF16, kind="ExternalInput").ap()
    alpha_d = nc.dram_tensor("alpha", [1, 1024], F32, kind="ExternalInput").ap()
    mask_d = nc.dram_tensor("mask", [128, 512], BF16, kind="ExternalInput").ap()
    onr_d = nc.dram_tensor("onesr", [1, 128], F32, kind="ExternalInput").ap()
    onc_d = nc.dram_tensor("onesc", [128, 1], F32, kind="ExternalInput").ap()
    outT = nc.dram_tensor("outT", [NT * NCH * 128, 512], F32,
                          kind="ExternalOutput").ap()
    spk = "ExternalOutput" if debug else "Internal"
    y1sp = nc.dram_tensor("y1sp", [2 * NCH * 128, 512], F32, kind=spk).ap()
    dsp = nc.dram_tensor("dsp", [2 * NCH * 128, 512], F32, kind=spk).ap()
    wqc = nc.dram_tensor("wqc", [8 * NCH * 128, 512], BF16, kind="Internal").ap()
    woc = nc.dram_tensor("woc", [8 * NCH * 128, 512], BF16, kind="Internal").ap()
    dbg = {}
    if debug:
        for nm, sh, dt_ in (("dln1", [128, NCH * 1024], BF16),
                            ("dkr", [128, 1024], BF16), ("dvt", [128, 1024], BF16),
                            ("dqr", [128, NCH * 512], BF16),
                            ("dao", [128, NCH * 512], BF16),
                            ("dO", [128, NCH * 512], F32),
                            ("dln2", [128, NCH * 512], BF16),
                            ("dh1", [128, NCH * 1024], BF16),
                            ("dh2", [128, NCH * 512], BF16)):
            dbg[nm] = nc.dram_tensor(nm, sh, dt_, kind="ExternalOutput").ap()

    es = ExitStack()
    sb = lambda n, sh, dt: es.enter_context(nc.sbuf_tensor(n, sh, dt))
    with es:
        A = sb("A", [128, NCH * 1024], BF16)
        A32 = A.bitcast(F32)
        Bq = sb("Bq", [128, NCH * 512], BF16)
        Cao = sb("Cao", [128, NCH * 512], BF16)
        KR = sb("KR", [128, 1024], BF16)
        VT = sb("VT", [128, 1024], BF16)
        SS = sb("SS", [128, 2048], BF16)
        QCo = sb("QCo", [128, 512], BF16)
        QSi = sb("QSi", [128, 512], BF16)
        KCo = sb("KCo", [128, 1024], BF16)
        KSi = sb("KSi", [128, 1024], BF16)
        STG = [sb(f"STG{i}", [128, 512], F32) for i in range(NSLAB)]
        STGI = [t.bitcast(I32) for t in STG]
        STB = [sb(f"STB{i}", [128, 512], BF16) for i in range(NSLAB)]
        XST = [sb(f"XST{i}", [128, 1024], F32) for i in range(2)]
        SQ = [sb(f"SQ{i}", [128, 1024], BF16) for i in range(2)]
        OBS = [SQ[i][:, 512:1024] for i in range(2)]
        SG = sb("SG", [128, 2048], BF16)
        SG32 = SG.bitcast(F32)
        XG = [sb(f"XG{i}", [128, 512], F32) for i in range(NXG)]
        RPS = [sb("RPS0", [128, 512], F32)]
        T12 = sb("T12", [128, 512], F32)
        MR = sb("MR", [1, 1024], F32)
        RR = sb("RR", [1, 1024], F32)
        ARW = sb("ARW", [1, 1024], F32)
        ALP = sb("ALP", [1, 1024], F32)
        MSK = sb("MSK", [128, 512], BF16)
        GSC = sb("GSCs", [128, FFCH], F32)
        USC = sb("USCs", [128, FFCH], F32)
        DSC = sb("DSCs", [128, NCH], F32)
        ONR = sb("ONRs", [1, 128], F32)
        ONC = sb("ONCs", [128, 1], F32)
        ONCB = sb("ONCBs", [128, 1], BF16)
        pb = [es.enter_context(nc.psum_tensor(f"pb{i}", [128, 512], F32))
              for i in range(8)]
        sems = {s: es.enter_context(nc.semaphore(s)) for s in SEMS}

        cnt = {s: 0 for s in SEMS}
        hw = {e: {s: 0 for s in SEMS} for e in ENGINES}
        q = {e: [] for e in ENGINES}

        def op(eng, fn, waits=(), inc=None, inc_by=1):
            if lockstep:
                waits = list(waits) + [(s, cnt[s]) for s in SEMS]
            ws = []
            for s, v in waits:
                if v > 0 and v > hw[eng][s]:
                    hw[eng][s] = v
                    ws.append((sems[s], v))
            tgt = None
            if inc is not None:
                cnt[inc] += inc_by
                tgt = (sems[inc], inc_by)

            def emit(e, fn=fn, ws=ws, tgt=tgt):
                for h_, v in ws:
                    e.wait_ge(h_, v)
                r = fn(e)
                if tgt is not None:
                    r.then_inc(*tgt)
            q[eng].append(emit)

        def barrier():
            snap = [(s, cnt[s]) for s in ("pe", "ac", "dv", "ad", "gp")]
            for e in ("tensor", "scalar", "vector", "gpsimd"):
                for s, v in snap:
                    if v > hw[e][s]:
                        hw[e][s] = v

                        def emit(eng_, h_=sems[s], v_=v):
                            eng_.wait_ge(h_, v_)
                        q[e].append(emit)

        ring = {"seq": 0, "uses": [0] * NSLAB,
                "stage_free": [("ac", 0)] * NSLAB,
                "wb_free": [("wb", 0)] * NSLAB,
                "bf_free": [0] * NSLAB}

        def slab(src_ap, cols, caster, is_int=False, slot=None):
            sl = ring["seq"] % NSLAB if slot is None else slot
            if slot is None:
                ring["seq"] += 1
            ring["uses"][sl] += 1
            wsem = "ws%d" % sl
            cs, cv = ring["stage_free"][sl]

            def dma_fn(e, sl=sl, c=cols, ii=is_int, s=src_ap):
                return e.dma_start(STGI[sl][:, 0:c] if ii else STG[sl][:, 0:c], s)
            op("sync", dma_fn, waits=[(cs, cv)], inc=wsem, inc_by=16)
            wd_v = cnt[wsem]
            csem = "dv" if caster == "vector" else "ac"

            if caster == "vector":
                def cast_fn(e, sl=sl, c=cols, ii=is_int):
                    return e.tensor_copy(STB[sl][:, 0:c],
                                         STGI[sl][:, 0:c] if ii else STG[sl][:, 0:c])
            else:
                def cast_fn(e, sl=sl, c=cols, ii=is_int):
                    return e.activation(STB[sl][:, 0:c],
                                        STGI[sl][:, 0:c] if ii else STG[sl][:, 0:c],
                                        AF.Copy, scale=f32(1.0))
            op(caster, cast_fn, waits=[(wsem, wd_v), ("pe", ring["bf_free"][sl]),
                                       ring["wb_free"][sl]],
               inc=csem)
            ring["stage_free"][sl] = (csem, cnt[csem])
            return sl, (csem, cnt[csem])

        def slab_done(sl):
            ring["bf_free"][sl] = cnt["pe"]

        def slab_direct(src_ap, cols, extra=()):
            sl = ring["seq"] % NSLAB
            ring["seq"] += 1
            wsem = "ws%d" % sl
            op("sync", lambda e, sl=sl, c=cols, s=src_ap:
                e.dma_start(STB[sl][:, 0:c], s),
               waits=[("pe", ring["bf_free"][sl]), ring["wb_free"][sl]]
                     + list(extra),
               inc=wsem, inc_by=16)
            return sl, (wsem, cnt[wsem])

        def dump(nm, src_ap):
            if not debug:
                return
            snap = [(s, cnt[s]) for s in ("pe", "ac", "dv", "ad", "gp")]
            op("gpsimd", lambda e, d=dbg[nm], s=src_ap: e.dma_start(d, s),
               waits=snap, inc="od", inc_by=16)
            odv = cnt["od"]
            for e in ("tensor", "scalar", "vector", "gpsimd"):
                hw[e]["od"] = odv

                def emit(eng_, h_=sems["od"], v_=odv):
                    eng_.wait_ge(h_, v_)
                q[e].append(emit)

        xst_free = [[("ac", 0)], [("ac", 0)]]

        def xdma(dst_slice, src_ap, slot, extra=()):
            ws = list(xst_free[slot]) + list(extra)
            op("scalar", lambda e, d=dst_slice, sa=src_ap: e.dma_start(d, sa),
               waits=ws, inc="xs%d" % slot, inc_by=16)
            return cnt["xs%d" % slot]

        xg = {"seq": 0, "free": [("dv", 0)] * NXG}

        def gdma(src_ap, extra=()):
            sl = xg["seq"] % NXG
            xg["seq"] += 1
            ws = [xg["free"][sl]] + list(extra)
            op("sync", lambda e, sl=sl, sa=src_ap: e.dma_start(XG[sl][:, :], sa),
               waits=ws, inc="xg%d" % sl, inc_by=16)
            return sl, cnt["xg%d" % sl]

        def ln_smalls(W, s_banks, q_banks, m_banks, r_banks, bank_free=()):
            nh = W // 512
            pe_stats = cnt["pe"]
            for h in range(nh):
                op("scalar", lambda e, h=h, b=s_banks[h]:
                    e.activation(MR[0:1, 512 * h:512 * h + 512], b[0:1, :], AF.Copy,
                                 scale=f32(1.0 / D)),
                   waits=[("pe", pe_stats)])
            for h in range(nh):
                op("scalar", lambda e, h=h:
                    e.activation(RR[0:1, 512 * h:512 * h + 512],
                                 MR[0:1, 512 * h:512 * h + 512],
                                 AF.Square, scale=f32(1.0)),
                   inc="ac" if h == nh - 1 else None)
            ac_m = cnt["ac"]
            for h in range(nh):
                op("tensor", lambda e, h=h, b=m_banks[h]:
                    e.matmul(b[:, :], ONR[0:1, :], MR[0:1, 512 * h:512 * h + 512],
                             start=True, stop=True),
                   waits=[("ac", ac_m)] + list(bank_free),
                   inc="pe" if h == nh - 1 else None)
            pe_mb = cnt["pe"]
            for h in range(nh):
                op("vector", lambda e, h=h, b=q_banks[h]:
                    e.scalar_tensor_tensor(MR[0:1, 512 * h:512 * h + 512], b[0:1, :],
                                           float(1.0 / D), RR[0:1, 512 * h:512 * h + 512],
                                           op0=ALU.mult, op1=ALU.subtract),
                   waits=[("pe", pe_mb), ("ac", ac_m)],
                   inc="dv" if h == nh - 1 else None)
            dv_var = cnt["dv"]
            for h in range(nh):
                op("scalar", lambda e, h=h:
                    e.activation(RR[0:1, 512 * h:512 * h + 512],
                                 MR[0:1, 512 * h:512 * h + 512],
                                 AF.Sqrt, bias=f32(EPS), scale=f32(1.0)),
                   waits=[("dv", dv_var)], inc="ac" if h == nh - 1 else None)
            ac_sq = cnt["ac"]
            op("vector", lambda e, W=W: e.reciprocal(MR[0:1, 0:W], RR[0:1, 0:W]),
               waits=[("ac", ac_sq)], inc="dv")
            dv_r = cnt["dv"]
            for h in range(nh):
                op("tensor", lambda e, h=h, b=r_banks[h]:
                    e.matmul(b[:, :], ONR[0:1, :], MR[0:1, 512 * h:512 * h + 512],
                             start=True, stop=True),
                   waits=[("dv", dv_r)], inc="pe" if h == nh - 1 else None)
            return cnt["pe"]

        for dst, src in ((GSC, gsc_d), (USC, usc_d), (DSC, dsc_d), (ALP, alpha_d),
                         (MSK, mask_d), (ONR, onr_d), (ONC, onc_d)):
            op("scalar", lambda e, d=dst, s=src: e.dma_start(d[:], s),
               inc="ad", inc_by=16)
        ad_init = cnt["ad"]
        op("vector", lambda e: e.tensor_copy(ONCB[:], ONC[:]), waits=[("ad", ad_init)],
           inc="dv")

        out_free = [[("os0", 0)], [("os1", 0)]]
        wbst = {"wq": 0, "wo": 0}
        p5st = {"set_sig": [("ac", 0)] * 3, "set_rs": [("dv", 0)] * 3,
                "set_bc": [("dv", 0)] * 3, "ss_use": [0, 0],
                "av_free": [0, 0], "arw_free": [0, 0], "msk_free": 0,
                "qk_pe": {}, "sig_ac": {}, "sig_dv": {}, "rs_pe": {},
                "av_pe": {}, "arw_dv": {}}

        for t in range(ntiles):
            h0 = 512 * t
            rope_dv_prev = cnt["dv"]
            pe_tile0 = cnt["pe"]
            pe_prev = wbst.get("pe_p10", 0) or pe_tile0
            dv_tile0 = cnt["dv"]
            ac_tile0 = cnt["ac"]

            op("scalar", lambda e, t=t: e.dma_start(QCo[:], qcos_d[:, T * t:T * t + T]),
               waits=[("dv", rope_dv_prev)], inc="ad", inc_by=16)
            op("scalar", lambda e, t=t: e.dma_start(QSi[:], qsin_d[:, T * t:T * t + T]),
               inc="ad", inc_by=16)
            op("scalar", lambda e, h0=h0: e.dma_start(KCo[:], kcos_d[:, h0:h0 + 1024]),
               inc="ad", inc_by=16)
            op("scalar", lambda e, h0=h0: e.dma_start(KSi[:], ksin_d[:, h0:h0 + 1024]),
               inc="ad", inc_by=16)
            ad_rope = cnt["ad"]

            sq_free = [pe_prev, pe_prev]
            for i in range(NCH):
                ad_v = xdma(XST[i % 2][:],
                            xp1[128 * (NCH * t + i):128 * (NCH * t + i) + 128, :],
                            i % 2)
                op("scalar", lambda e, i=i: e.activation(A[:, 1024 * i:1024 * i + 1024],
                                                         XST[i % 2][:], AF.Copy,
                                                         scale=f32(1.0)),
                   waits=[("xs%d" % (i % 2), ad_v), ("pe", sq_free[i % 2]),
                          ("pe", pe_prev)], inc="ac")
                ac_v = cnt["ac"]
                op("scalar", lambda e, i=i: e.activation(SQ[i % 2][:], XST[i % 2][:],
                                                         AF.Square, scale=f32(1.0)),
                   inc="gp")
                gp_v = cnt["gp"]
                xst_free[i % 2] = [("ac", ac_v), ("gp", gp_v)]
                for hh in range(2):
                    op("tensor", lambda e, i=i, hh=hh:
                        e.matmul(pb[hh][0:1, :], ONCB[:],
                                 A[:, 1024 * i + 512 * hh:1024 * i + 512 * hh + 512],
                                 start=(i == 0), stop=(i == NCH - 1)),
                       waits=[("ac", ac_v)])
                for hh in range(2):
                    op("tensor", lambda e, i=i, hh=hh:
                        e.matmul(pb[2 + hh][0:1, :], ONCB[:],
                                 SQ[i % 2][:, 512 * hh:512 * hh + 512],
                                 start=(i == 0), stop=(i == NCH - 1)),
                       waits=[("gp", gp_v)], inc="pe" if hh == 1 else None)
                sq_free[i % 2] = cnt["pe"]

            pe_bc1 = ln_smalls(1024, [pb[0], pb[1]], [pb[2], pb[3]],
                               [pb[4], pb[5]], [pb[6], pb[7]],
                               bank_free=[("dv", dv_tile0), ("ac", ac_tile0)])
            for i in range(NCH):
                for hh in range(2):
                    op("vector", lambda e, i=i, hh=hh:
                        e.tensor_tensor(XST[0][:, 512 * hh:512 * hh + 512],
                                        A[:, 1024 * i + 512 * hh:1024 * i + 512 * hh + 512],
                                        pb[4 + hh][:, :], op=ALU.subtract),
                       waits=[("pe", pe_bc1)])
                for hh in range(2):
                    op("vector", lambda e, i=i, hh=hh:
                        e.tensor_tensor(A[:, 1024 * i + 512 * hh:1024 * i + 512 * hh + 512],
                                        XST[0][:, 512 * hh:512 * hh + 512],
                                        pb[6 + hh][:, :], op=ALU.mult),
                       inc="dv" if hh == 1 else None)
            dv_ln1 = cnt["dv"]
            xst_free[0] = [("dv", dv_ln1)]
            barrier()
            if t == 0:
                dump("dln1", A[:])

            ac_v0 = 0
            for vpass in range(2):
                for i in range(NCH):
                    if vpass == 0:
                        slk, rk = slab(wkT[128 * i:128 * i + 128, :], 128, "scalar")
                    slv, rv = slab(wvT[128 * i:128 * i + 128, :], 128, "scalar")
                    if vpass == 0:
                        for hh in range(2):
                            op("tensor", lambda e, i=i, hh=hh, sl=slk:
                                e.matmul(pb[hh][:, :], STB[sl][:, 0:128],
                                         A[:, 1024 * i + 512 * hh:1024 * i + 512 * hh + 512],
                                         start=(i == 0), stop=(i == NCH - 1)),
                               waits=[rk, ("dv", dv_ln1)],
                               inc="pe" if hh == 1 else None)
                        slab_done(slk)
                    for kc in range(4):
                        ck = 4 * vpass + kc
                        op("tensor", lambda e, i=i, kc=kc, ck=ck, sl=slv:
                            e.matmul(pb[2 + kc][:, 0:128],
                                     A[:, 1024 * i + 128 * ck:1024 * i + 128 * ck + 128],
                                     STB[sl][:, 0:128],
                                     start=(i == 0), stop=(i == NCH - 1)),
                           waits=[rv, ("dv", dv_ln1), ("ac", ac_v0)],
                           inc="pe" if kc == 3 else None)
                    slab_done(slv)
                pe_v = cnt["pe"]
                for kc in range(4):
                    ck = 4 * vpass + kc
                    op("scalar", lambda e, kc=kc, ck=ck:
                        e.activation(VT[:, 128 * ck:128 * ck + 128], pb[2 + kc][:, 0:128],
                                     AF.Copy, scale=f32(1.0)),
                       waits=[("pe", pe_v)], inc="ac" if kc == 3 else None)
                ac_v0 = cnt["ac"]
                if vpass == 0:
                    pe_k = pe_v
                    for hh in range(2):
                        c0 = 512 * hh
                        op("scalar", lambda e, hh=hh, c0=c0: e.activation(
                            XST[1][0:64, c0:c0 + 512], pb[hh][64:128, :], AF.Copy,
                            scale=f32(1.0)),
                           waits=[("pe", pe_k)])
                        op("scalar", lambda e, hh=hh, c0=c0: e.activation(
                            XST[1][64:128, c0:c0 + 512], pb[hh][0:64, :], AF.Copy,
                            scale=f32(1.0)),
                           inc="ac")
                        ac_rot = cnt["ac"]
                        op("vector", lambda e, hh=hh, c0=c0: e.tensor_tensor(
                            XST[0][:, c0:c0 + 512], XST[1][:, c0:c0 + 512],
                            KSi[:, c0:c0 + 512], op=ALU.mult),
                           waits=[("ac", ac_rot), ("ad", ad_rope)])
                        op("vector", lambda e, hh=hh, c0=c0: e.tensor_tensor(
                            XST[1][:, c0:c0 + 512], pb[hh][:, :],
                            KCo[:, c0:c0 + 512], op=ALU.mult))
                        op("vector", lambda e, hh=hh, c0=c0: e.tensor_tensor(
                            KR[:, c0:c0 + 512], XST[0][:, c0:c0 + 512],
                            XST[1][:, c0:c0 + 512], op=ALU.add),
                           inc="dv")
                    xst_free[0] = [("dv", cnt["dv"])]
                    xst_free[1] = [("dv", cnt["dv"])]
            barrier()
            if t == 0:
                dump("dkr", KR[:])
                dump("dvt", VT[:])

            qrope_free = [0, 0]
            for og in range(8):
                pbase = (og % 2) * 4
                for i in range(NCH):
                    bi = 128 * (NCH * og + i)
                    if t == 0:
                        sl, r = slab(wqb[bi:bi + 128, :], 512, "scalar")
                        op("scalar", lambda e, sl=sl, bi=bi: e.dma_start(
                            wqc[bi:bi + 128, :], STB[sl][:, 0:512]),
                           waits=[r], inc="wb", inc_by=16)
                        ring["wb_free"][sl] = ("wb", cnt["wb"])
                    else:
                        sl, r = slab_direct(
                            wqc[bi:bi + 128, :], 512,
                            extra=([("wb", wbst["wq"])] if t == 1 else ()))
                    for j in range(4):
                        op("tensor", lambda e, i=i, j=j, sl=sl, pbase=pbase:
                            e.matmul(pb[pbase + j][:, :], STB[sl][:, 128 * j:128 * j + 128],
                                     A[:, 1024 * i + 256:1024 * i + 768],
                                     start=(i == 0), stop=(i == NCH - 1)),
                           waits=[r, ("dv", qrope_free[og % 2])],
                           inc="pe" if j == 3 else None)
                    slab_done(sl)
                pe_q = cnt["pe"]
                for j in range(4):
                    hq = 4 * og + j
                    bk = pb[pbase + j]
                    op("scalar", lambda e, bk=bk: e.activation(
                        XST[0][0:64, 0:512], bk[64:128, :], AF.Copy, scale=f32(1.0)),
                       waits=[("pe", pe_q), ("dv", cnt["dv"])])
                    op("scalar", lambda e, bk=bk: e.activation(
                        XST[0][64:128, 0:512], bk[0:64, :], AF.Copy, scale=f32(1.0)),
                       inc="ac")
                    ac_rot = cnt["ac"]
                    op("vector", lambda e: e.tensor_tensor(
                        XST[0][:, 512:1024], XST[0][:, 0:512], QSi[:, :], op=ALU.mult),
                       waits=[("ac", ac_rot), ("ad", ad_rope)])
                    op("vector", lambda e, bk=bk: e.tensor_tensor(
                        XST[0][:, 0:512], bk[:, :], QCo[:, :], op=ALU.mult))
                    op("vector", lambda e, hq=hq: e.tensor_tensor(
                        Bq[:, 512 * hq:512 * hq + 512], XST[0][:, 512:1024],
                        XST[0][:, 0:512], op=ALU.add),
                       inc="dv")
                qrope_free[og % 2] = cnt["dv"]
            xst_free[0] = [("dv", cnt["dv"])]
            if t == 0:
                wbst["wq"] = cnt["wb"]
            barrier()
            if t == 0:
                dump("dqr", Bq[:])

            SCB = ((0, 1), (2, 3), (4, 5))

            def p5_qk(s):
                bh, w = s // 2, s % 2
                b, hq = bh % 2, bh // 2
                qc0 = 512 * hq + 256 * b
                kb = 256 * b + 256 * w
                st = s % 3
                ws = [p5st["set_sig"][st], p5st["set_rs"][st], p5st["set_bc"][st]]
                for kc in range(4):
                    bank = pb[SCB[st][kc // 2]]
                    cc = 256 * (kc % 2)
                    op("tensor", lambda e, bank=bank, cc=cc, kb=kb, kc=kc, qc0=qc0:
                        e.matmul(bank[:, cc:cc + 256],
                                 KR[:, kb + 128 * kc:kb + 128 * kc + 128],
                                 Bq[:, qc0:qc0 + 256], start=True, stop=True),
                       waits=ws if kc == 0 else (),
                       inc="pe" if kc == 3 else None)
                p5st["qk_pe"][s] = cnt["pe"]

            def p5_sig(s):
                bh, w = s // 2, s % 2
                b = bh % 2
                st = s % 3
                masked = (t == NT - 1 and b == 1 and w == 1)
                base = 1024 * (s % 2)
                ws = [("pe", p5st["qk_pe"][s]), ("pe", p5st["ss_use"][s % 2])]
                op("scalar", lambda e, st=st, base=base:
                    e.activation(SS[:, base:base + 512], pb[SCB[st][0]][:, :],
                                 AF.Sigmoid, scale=f32(ATT_SCALE)),
                   waits=ws)
                if masked:
                    op("scalar", lambda e, st=st:
                        e.activation(SQ[0][:, 0:512], pb[SCB[st][1]][:, :],
                                     AF.Sigmoid, scale=f32(ATT_SCALE)),
                       waits=[("dv", p5st["msk_free"])], inc="ac")
                    op("vector", lambda e, base=base:
                        e.tensor_tensor(SS[:, base + 512:base + 1024],
                                        SQ[0][:, 0:512], MSK[:, :], op=ALU.mult),
                       waits=[("ac", cnt["ac"])], inc="dv")
                    p5st["msk_free"] = cnt["dv"]
                    p5st["sig_dv"][s] = cnt["dv"]
                else:
                    op("scalar", lambda e, st=st, base=base:
                        e.activation(SS[:, base + 512:base + 1024],
                                     pb[SCB[st][1]][:, :], AF.Sigmoid,
                                     scale=f32(ATT_SCALE)),
                       inc="ac")
                    p5st["sig_dv"][s] = 0
                p5st["sig_ac"][s] = cnt["ac"]
                p5st["set_sig"][st] = ("ac", cnt["ac"])

            def p5_rs_av(s):
                bh, w = s // 2, s % 2
                b = bh % 2
                kb = 256 * b + 256 * w
                st = s % 3
                base = 1024 * (s % 2)
                ws = [("ac", p5st["sig_ac"][s])]
                if p5st["sig_dv"][s]:
                    ws.append(("dv", p5st["sig_dv"][s]))
                rsb = pb[SCB[st][0]]
                for kc in range(4):
                    op("tensor", lambda e, rsb=rsb, base=base, kc=kc:
                        e.matmul(rsb[0:1, 0:256], ONCB[:],
                                 SS[:, base + 256 * kc:base + 256 * kc + 256],
                                 start=(kc == 0), stop=(kc == 3)),
                       waits=ws if kc == 0 else (),
                       inc="pe" if kc == 3 else None)
                p5st["rs_pe"][s] = cnt["pe"]
                avb = pb[6 + (s // 2) % 2]
                avc = 256 * (s % 2)
                avw = [("dv", p5st["av_free"][(s // 2) % 2])]
                for kc in range(4):
                    ck = kb // 128 + kc
                    op("tensor", lambda e, avb=avb, avc=avc, ck=ck, base=base, kc=kc:
                        e.matmul(avb[:, avc:avc + 256],
                                 VT[:, 128 * ck:128 * ck + 128],
                                 SS[:, base + 256 * kc:base + 256 * kc + 256],
                                 start=(kc == 0), stop=(kc == 3)),
                       waits=avw if kc == 0 else (),
                       inc="pe" if kc == 3 else None)
                p5st["av_pe"][s] = cnt["pe"]
                p5st["ss_use"][s % 2] = cnt["pe"]

            def p5_vec_post(s):
                bh, w = s // 2, s % 2
                b = bh % 2
                st = s % 3
                k = s // 2
                arwc = 512 * (k % 2) + 256 * w
                sel = (0 if (t == 0 and b == 0) else 512) + 256 * w
                rsb = pb[SCB[st][0]]
                rrc = 256 * (s % 4)
                op("vector", lambda e, rsb=rsb, rrc=rrc:
                    e.reciprocal(RR[0:1, rrc:rrc + 256], rsb[0:1, 0:256]),
                   waits=[("pe", p5st["rs_pe"][s])])
                op("vector", lambda e, arwc=arwc, sel=sel, rrc=rrc:
                    e.tensor_tensor(ARW[0:1, arwc:arwc + 256],
                                    RR[0:1, rrc:rrc + 256],
                                    ALP[0:1, sel:sel + 256], op=ALU.mult),
                   waits=[("pe", p5st["arw_free"][k % 2]), ("ad", ad_init)],
                   inc="dv")
                p5st["set_rs"][st] = ("dv", cnt["dv"])
                p5st["arw_dv"][s] = cnt["dv"]

            def p5_bc_blend(k):
                st = (2 * k + 2) % 3
                bcb = pb[SCB[st][1]]
                arw0 = 512 * (k % 2)
                op("tensor", lambda e, bcb=bcb, arw0=arw0:
                    e.matmul(bcb[:, :], ONR[0:1, :], ARW[0:1, arw0:arw0 + 512],
                             start=True, stop=True),
                   waits=[("dv", p5st["arw_dv"][2 * k + 1]),
                          p5st["set_sig"][st], p5st["set_bc"][st]],
                   inc="pe")
                p5st["arw_free"][k % 2] = cnt["pe"]
                bc_pe = cnt["pe"]
                op("vector", lambda e, bcb=bcb: e.tensor_copy(RPS[0][:, :], bcb[:, :]),
                   waits=[("pe", bc_pe)])
                avb = pb[6 + (k % 2)]
                op("vector", lambda e, avb=avb: e.tensor_tensor(
                    T12[:, 0:256], avb[:, 0:256], RPS[0][:, 0:256], op=ALU.mult),
                   waits=[("pe", p5st["av_pe"][2 * k])])
                op("vector", lambda e, avb=avb: e.tensor_tensor(
                    T12[:, 256:512], avb[:, 256:512], RPS[0][:, 256:512], op=ALU.mult),
                   waits=[("pe", p5st["av_pe"][2 * k + 1])])
                b, hq = k % 2, k // 2
                op("vector", lambda e, hq=hq, b=b: e.tensor_tensor(
                    Cao[:, 512 * hq + 256 * b:512 * hq + 256 * b + 256],
                    T12[:, 0:256], T12[:, 256:512], op=ALU.add),
                   inc="dv")
                p5st["av_free"][k % 2] = cnt["dv"]
                p5st["set_bc"][st] = ("dv", cnt["dv"])

            NSLOT = 128
            for s in range(NSLOT + 2):
                if s < NSLOT:
                    p5_qk(s)
                    p5_sig(s)
                if 1 <= s <= NSLOT:
                    p5_rs_av(s - 1)
                    p5_vec_post(s - 1)
                if s >= 3 and s % 2 == 1:
                    p5_bc_blend((s - 3) // 2)
            barrier()
            if t == 0:
                dump("dao", Cao[:])

            SETS = ((0, 1), (2, 3), (6, 7))
            set_free = [("ac", 0)] * 3
            obs_free = [0, 0]
            for og in range(8):
                sA, sB = SETS[(2 * og) % 3], SETS[(2 * og + 1) % 3]
                bkof = [sA[0], sA[1], sB[0], sB[1]]
                for i in range(NCH):
                    bi = 128 * (NCH * og + i)
                    if t == 0:
                        sl, r = slab(wob[bi:bi + 128, :], 512, "scalar")
                        op("scalar", lambda e, sl=sl, bi=bi: e.dma_start(
                            woc[bi:bi + 128, :], STB[sl][:, 0:512]),
                           waits=[r], inc="wb", inc_by=16)
                        ring["wb_free"][sl] = ("wb", cnt["wb"])
                    else:
                        sl, r = slab_direct(
                            woc[bi:bi + 128, :], 512,
                            extra=([("wb", wbst["wo"])] if t == 1 else ()))
                    for j in range(4):
                        op("tensor", lambda e, i=i, j=j, sl=sl, bk=bkof[j]:
                            e.matmul(pb[bk][:, :], STB[sl][:, 128 * j:128 * j + 128],
                                     Cao[:, 512 * i:512 * i + 512],
                                     start=(i == 0), stop=(i == NCH - 1)),
                           waits=[r, set_free[(2 * og + j // 2) % 3]],
                           inc="pe" if j == 3 else None)
                    slab_done(sl)
                pe_wo = cnt["pe"]
                for j in range(4):
                    o = 4 * og + j
                    op("scalar", lambda e, j=j, o=o, bk=bkof[j]: e.activation(
                        A32[:, 512 * o:512 * o + 512], pb[bk][:, :], AF.Copy,
                        scale=f32(1.0)),
                       waits=[("pe", pe_wo)], inc="ac")
                    ac_e = cnt["ac"]
                    if j == 1:
                        set_free[(2 * og) % 3] = ("ac", ac_e)
                    if j == 3:
                        set_free[(2 * og + 1) % 3] = ("ac", ac_e)
                    op("vector", lambda e, o=o: e.tensor_copy(
                        OBS[o % 2][:, :], A32[:, 512 * o:512 * o + 512]),
                       waits=[("ac", ac_e), ("pe", obs_free[o % 2])])
                    op("vector", lambda e, o=o: e.tensor_tensor(
                        SQ[o % 2][:, 0:512], A32[:, 512 * o:512 * o + 512],
                        A32[:, 512 * o:512 * o + 512], op=ALU.mult),
                       inc="dv")
                    dv_e = cnt["dv"]
                    op("tensor", lambda e, o=o: e.matmul(pb[4][0:1, :], ONCB[:],
                                                         OBS[o % 2][:, :],
                                                         start=(o == 0), stop=(o == NCH - 1)),
                       waits=[("dv", dv_e)])
                    op("tensor", lambda e, o=o: e.matmul(pb[5][0:1, :], ONCB[:],
                                                         SQ[o % 2][:, 0:512],
                                                         start=(o == 0), stop=(o == NCH - 1)),
                       inc="pe")
                    obs_free[o % 2] = cnt["pe"]
            if t == 0:
                wbst["wo"] = cnt["wb"]
                dump("dO", A32[:])
            pe_bcO = ln_smalls(512, [pb[4]], [pb[5]], [pb[6]], [pb[7]],
                               bank_free=[set_free[2]])

            sg_free = [list(out_free[0]), list(out_free[1])]
            y1_evt = []
            for o in range(NCH):
                sl, gv = gdma(xp7[128 * (NCH * t + o):128 * (NCH * t + o) + 128, :])
                sgs = o % 2
                op("vector", lambda e, o=o: e.tensor_tensor(
                    T12[:, :], A32[:, 512 * o:512 * o + 512], pb[6][:, :],
                    op=ALU.subtract),
                   waits=[("xg%d" % sl, gv), ("pe", pe_bcO)] + sg_free[sgs])
                op("vector", lambda e: e.tensor_tensor(
                    T12[:, :], T12[:, :], pb[7][:, :], op=ALU.mult))
                op("vector", lambda e, o=o, sgs=sgs, sl=sl: e.tensor_tensor(
                    SG32[:, 512 * sgs:512 * sgs + 512], T12[:, :], XG[sl][:, :],
                    op=ALU.add),
                   inc="dv")
                dv_y1 = cnt["dv"]
                xg["free"][sl] = ("dv", dv_y1)
                op("scalar", lambda e, sgs=sgs: e.activation(
                    OBS[sgs][:, :], SG32[:, 512 * sgs:512 * sgs + 512], AF.Copy,
                    scale=f32(1.0)),
                   waits=[("dv", dv_y1)])
                op("scalar", lambda e, sgs=sgs: e.activation(
                    SQ[sgs][:, 0:512], SG32[:, 512 * sgs:512 * sgs + 512], AF.Square,
                    scale=f32(1.0)),
                   inc="ac")
                ac_y = cnt["ac"]
                op("tensor", lambda e, o=o, sgs=sgs: e.matmul(
                    pb[0][0:1, :], ONCB[:], OBS[sgs][:, :],
                    start=(o == 0), stop=(o == NCH - 1)),
                   waits=[("ac", ac_y)])
                op("tensor", lambda e, o=o, sgs=sgs: e.matmul(
                    pb[1][0:1, :], ONCB[:], SQ[sgs][:, 0:512],
                    start=(o == 0), stop=(o == NCH - 1)),
                   inc="pe")
                op("scalar", lambda e, o=o, t=t, sgs=sgs: e.dma_start(
                    y1sp[128 * (NCH * (t % 2) + o):128 * (NCH * (t % 2) + o) + 128, :],
                    SG32[:, 512 * sgs:512 * sgs + 512]),
                   waits=[("dv", dv_y1)], inc="sg%d" % sgs, inc_by=16)
                y1_evt.append(("sg%d" % sgs, cnt["sg%d" % sgs]))
                sg_free[sgs] = [("sg%d" % sgs, cnt["sg%d" % sgs]), ("ac", ac_y)]
            y1_all = [("sg0", cnt["sg0"]), ("sg1", cnt["sg1"])]
            pe_bc2 = ln_smalls(512, [pb[0]], [pb[1]], [pb[2]], [pb[3]])

            for i in range(NCH):
                sl, gv = gdma(y1sp[128 * (NCH * (t % 2) + i):
                                   128 * (NCH * (t % 2) + i) + 128, :],
                              extra=y1_all)
                op("vector", lambda e, sl=sl: e.tensor_tensor(
                    T12[:, :], XG[sl][:, :], pb[2][:, :], op=ALU.subtract),
                   waits=[("xg%d" % sl, gv), ("pe", pe_bc2)])
                op("vector", lambda e, i=i: e.tensor_tensor(
                    Bq[:, 512 * i:512 * i + 512], T12[:, :], pb[3][:, :], op=ALU.mult),
                   inc="dv")
                xg["free"][sl] = ("dv", cnt["dv"])
            barrier()
            if t == 0:
                dump("dln2", Bq[:])

            silu_free = [("dv", 0)] * 4
            up_free = [("dv", 0)] * 4
            gate_free = [("ac", 0)] * 4
            GS = [(g2, i2) for g2 in range(22) for i2 in range(NCH)]
            gready = {}
            gst = {"n": 0}

            def gpump(upto):
                while gst["n"] < len(GS) and gst["n"] < upto:
                    k = gst["n"]
                    g2, i2 = GS[k]
                    bi = 128 * (NCH * g2 + i2)
                    gready[k] = slab(gb[bi:bi + 128, :], 512, "vector",
                                     is_int=True, slot=k % 4)
                    gst["n"] += 1

            gpump(4)
            for g in range(22):
                gw = min(512, DFF - 512 * g)
                nch_g = gw // 128
                for i in range(NCH):
                    k = NCH * g + i
                    sl, r = gready.pop(k)
                    for j in range(nch_g):
                        op("tensor", lambda e, i=i, j=j, sl=sl:
                            e.matmul(pb[j][:, :], STB[sl][:, 128 * j:128 * j + 128],
                                     Bq[:, 512 * i:512 * i + 512],
                                     start=(i == 0), stop=(i == NCH - 1)),
                           waits=[r, gate_free[j]],
                           inc="pe" if j == nch_g - 1 else None)
                    slab_done(sl)
                    gpump(k + 4)
                pe_g = cnt["pe"]
                for j in range(nch_g):
                    c = 4 * g + j
                    op("scalar", lambda e, j=j, c=c: e.activation(
                        SG[:, 512 * j:512 * j + 512], pb[j][:, :], AF.Silu,
                        scale=GSC[:, c:c + 1]),
                       waits=[("pe", pe_g), silu_free[j]], inc="ac")
                    gate_free[j] = ("ac", cnt["ac"])
                ac_silu = cnt["ac"]
                for i in range(NCH):
                    bi = 128 * (NCH * g + i)
                    sl, r = slab(ub[bi:bi + 128, :], 512, "scalar", is_int=True,
                                 slot=4 + i % 2)
                    for j in range(nch_g):
                        op("tensor", lambda e, i=i, j=j, sl=sl:
                            e.matmul(pb[4 + j][:, :], STB[sl][:, 128 * j:128 * j + 128],
                                     Bq[:, 512 * i:512 * i + 512],
                                     start=(i == 0), stop=(i == NCH - 1)),
                           waits=[r, up_free[j]],
                           inc="pe" if j == nch_g - 1 else None)
                    slab_done(sl)
                pe_u = cnt["pe"]
                for j in range(nch_g):
                    c = 4 * g + j
                    dst = (A, 512 * c) if c < 64 else (Cao, 512 * (c - 64))
                    op("vector", lambda e, j=j, c=c, dst=dst: e.scalar_tensor_tensor(
                        dst[0][:, dst[1]:dst[1] + 512], pb[4 + j][:, :], USC[:, c:c + 1],
                        SG[:, 512 * j:512 * j + 512], op0=ALU.mult, op1=ALU.mult),
                       waits=[("pe", pe_u), ("ac", ac_silu)], inc="dv")
                    up_free[j] = ("dv", cnt["dv"])
                    silu_free[j] = ("dv", cnt["dv"])
            barrier()
            if t == 0:
                dump("dh1", A[:])
                dump("dh2", Cao[:])

            SETS = ((0, 1), (2, 3), (6, 7))
            set_free = [("ac", 0)] * 3
            obs_free = [0, 0]
            d_evt = []
            dsg_free = [[("ad", 0)], [("ad", 0)]]
            for dg in range(8):
                sA, sB = SETS[(2 * dg) % 3], SETS[(2 * dg + 1) % 3]
                bkof = [sA[0], sA[1], sB[0], sB[1]]
                for ff in range(FFCH):
                    bi = 128 * (FFCH * dg + ff)
                    sl, r = slab(db[bi:bi + 128, :], 512, "vector", is_int=True)
                    src = (A, 512 * ff) if ff < 64 else (Cao, 512 * (ff - 64))
                    for j in range(4):
                        op("tensor", lambda e, ff=ff, j=j, sl=sl, src=src, bk=bkof[j]:
                            e.matmul(pb[bk][:, :], STB[sl][:, 128 * j:128 * j + 128],
                                     src[0][:, src[1]:src[1] + 512],
                                     start=(ff == 0), stop=(ff == FFCH - 1)),
                           waits=[r, set_free[(2 * dg + j // 2) % 3]],
                           inc="pe" if j == 3 else None)
                    slab_done(sl)
                pe_d = cnt["pe"]
                for j in range(4):
                    o = 4 * dg + j
                    sgs = o % 2
                    op("scalar", lambda e, j=j, o=o, sgs=sgs, bk=bkof[j]: e.activation(
                        SG32[:, 512 * sgs:512 * sgs + 512], pb[bk][:, :], AF.Copy,
                        scale=DSC[:, o:o + 1]),
                       waits=[("pe", pe_d)] + dsg_free[sgs], inc="ac")
                    ac_e = cnt["ac"]
                    if j == 1:
                        set_free[(2 * dg) % 3] = ("ac", ac_e)
                    if j == 3:
                        set_free[(2 * dg + 1) % 3] = ("ac", ac_e)
                    op("vector", lambda e, sgs=sgs, o=o: e.tensor_copy(
                        OBS[o % 2][:, :], SG32[:, 512 * sgs:512 * sgs + 512]),
                       waits=[("ac", ac_e), ("pe", obs_free[o % 2])])
                    op("vector", lambda e, sgs=sgs, o=o: e.tensor_tensor(
                        SQ[o % 2][:, 0:512], SG32[:, 512 * sgs:512 * sgs + 512],
                        SG32[:, 512 * sgs:512 * sgs + 512], op=ALU.mult),
                       inc="dv")
                    dv_e = cnt["dv"]
                    op("tensor", lambda e, o=o: e.matmul(pb[4][0:1, :], ONCB[:],
                                                         OBS[o % 2][:, :],
                                                         start=(o == 0), stop=(o == NCH - 1)),
                       waits=[("dv", dv_e)])
                    op("tensor", lambda e, o=o: e.matmul(pb[5][0:1, :], ONCB[:],
                                                         SQ[o % 2][:, 0:512],
                                                         start=(o == 0), stop=(o == NCH - 1)),
                       inc="pe")
                    obs_free[o % 2] = cnt["pe"]
                    op("scalar", lambda e, o=o, t=t, sgs=sgs: e.dma_start(
                        dsp[128 * (NCH * (t % 2) + o):
                            128 * (NCH * (t % 2) + o) + 128, :],
                        SG32[:, 512 * sgs:512 * sgs + 512]),
                       waits=[("ac", ac_e)], inc="sg%d" % sgs, inc_by=16)
                    d_evt.append(("sg%d" % sgs, cnt["sg%d" % sgs]))
                    dsg_free[sgs] = [("sg%d" % sgs, cnt["sg%d" % sgs]),
                                     ("dv", dv_e)]
            d_all = [("sg0", cnt["sg0"]), ("sg1", cnt["sg1"])]
            wbst["pe_p10"] = cnt["pe"]
            pe_bc3 = ln_smalls(512, [pb[4]], [pb[5]], [pb[6]], [pb[7]],
                               bank_free=[set_free[2]])

            for o in range(NCH):
                sl_d, gv_d = gdma(dsp[128 * (NCH * (t % 2) + o):
                                      128 * (NCH * (t % 2) + o) + 128, :],
                                  extra=d_all)
                sl_y, gv_y = gdma(y1sp[128 * (NCH * (t % 2) + o):
                                       128 * (NCH * (t % 2) + o) + 128, :])
                sgs = o % 2
                op("vector", lambda e, sl_d=sl_d: e.tensor_tensor(
                    T12[:, :], XG[sl_d][:, :], pb[6][:, :], op=ALU.subtract),
                   waits=[("xg%d" % sl_d, gv_d), ("xg%d" % sl_y, gv_y), ("pe", pe_bc3)] + dsg_free[sgs]
                         + out_free[sgs])
                op("vector", lambda e: e.tensor_tensor(
                    T12[:, :], T12[:, :], pb[7][:, :], op=ALU.mult))
                op("vector", lambda e, sl_y=sl_y, sgs=sgs: e.tensor_tensor(
                    SG32[:, 512 * sgs:512 * sgs + 512], T12[:, :], XG[sl_y][:, :],
                    op=ALU.add),
                   inc="dv")
                dv_o = cnt["dv"]
                xg["free"][sl_d] = ("dv", dv_o)
                xg["free"][sl_y] = ("dv", dv_o)
                op("scalar", lambda e, o=o, t=t, sgs=sgs: e.dma_start(
                    outT[128 * (NCH * t + o):128 * (NCH * t + o) + 128, :],
                    SG32[:, 512 * sgs:512 * sgs + 512]),
                   waits=[("dv", dv_o)], inc="os%d" % sgs, inc_by=16)
                out_free[sgs] = [("os%d" % sgs, cnt["os%d" % sgs])]

        op("gpsimd", lambda e: e.engine_nop(),
           waits=[("os0", cnt["os0"]), ("os1", cnt["os1"])])

        with nc.Block() as block:
            @block.sync
            def _(e):
                for fn in q["sync"]:
                    fn(e)

            @block.tensor
            def _(e):
                for fn in q["tensor"]:
                    fn(e)

            @block.scalar
            def _(e):
                for fn in q["scalar"]:
                    fn(e)

            @block.vector
            def _(e):
                for fn in q["vector"]:
                    fn(e)

            @block.gpsimd
            def _(e):
                for fn in q["gpsimd"]:
                    fn(e)

    return nc


def _rope_tables():
    inv = 1.0 / (10000.0 ** (np.arange(0, DH, 2, dtype=np.float32) / DH))
    tt = np.arange(S, dtype=np.float32)
    f = np.outer(tt, inv)
    cos = np.zeros((S, DH), dtype=np.float32)
    sin = np.zeros((S, DH), dtype=np.float32)
    cos[:, 0::2] = np.cos(f)
    cos[:, 1::2] = np.cos(f)
    sin[:, 0::2] = np.sin(f)
    sin[:, 1::2] = np.sin(f)
    return cos, sin


_NC_CACHE = {}


def make_inmaps(L0_attn_x, L0_attn_Wq, L0_attn_Wk, L0_attn_Wv, L0_attn_Wo,
                L0_ffn_gate_w, L0_ffn_gate_s, L0_ffn_up_w, L0_ffn_up_s,
                L0_ffn_down_w, L0_ffn_down_s):
    x = np.asarray(L0_attn_x, dtype=np.float32)
    wqT = np.ascontiguousarray(np.asarray(L0_attn_Wq, np.float32).T)
    wkT = np.ascontiguousarray(np.asarray(L0_attn_Wk, np.float32).T)
    wvT = np.ascontiguousarray(np.asarray(L0_attn_Wv, np.float32).T)
    woT = np.ascontiguousarray(np.asarray(L0_attn_Wo, np.float32).T)
    gT = np.ascontiguousarray(np.asarray(L0_ffn_gate_w, np.int32).T)
    uT = np.ascontiguousarray(np.asarray(L0_ffn_up_w, np.int32).T)
    dT = np.ascontiguousarray(np.asarray(L0_ffn_down_w, np.int32).T)
    gsc = np.ascontiguousarray(np.asarray(L0_ffn_gate_s, np.float32).reshape(FFCH, 128).T)
    usc = np.ascontiguousarray(np.asarray(L0_ffn_up_s, np.float32).reshape(FFCH, 128).T)
    dsc = np.ascontiguousarray(np.asarray(L0_ffn_down_s, np.float32).reshape(NCH, 128).T)

    def _blk_og(w):
        return np.ascontiguousarray(
            w.reshape(NCH, 128, 8, 512).transpose(2, 0, 1, 3).reshape(-1, 512))
    wqb = _blk_og(wqT)
    wob = _blk_og(woT)
    gpad = np.zeros((D, 22 * 512), np.int32)
    gpad[:, :DFF] = gT
    gbb = np.ascontiguousarray(
        gpad.reshape(NCH, 128, 22, 512).transpose(2, 0, 1, 3).reshape(-1, 512))
    gpad[:, :DFF] = uT
    ubb = np.ascontiguousarray(
        gpad.reshape(NCH, 128, 22, 512).transpose(2, 0, 1, 3).reshape(-1, 512))
    dbb = np.ascontiguousarray(
        dT.reshape(FFCH, 128, 8, 512).transpose(2, 0, 1, 3).reshape(-1, 512))

    cos, sin = _rope_tables()
    sgn = np.concatenate([-np.ones(64, np.float32), np.ones(64, np.float32)])
    ramp = np.linspace(0.0, 1.0, 256, dtype=np.float32)
    normal = np.concatenate([1.0 - ramp, ramp]).astype(np.float32)
    first = np.concatenate([np.zeros(256, np.float32), np.ones(256, np.float32)])
    onr = np.ones((1, 128), np.float32)
    onc = np.ones((128, 1), np.float32)

    xflat = x.reshape(B * S, D)
    in_maps = []
    for c in range(NCORES):
        seq, chunk = divmod(c, 4)
        lo = chunk * OWN - 256
        hi = chunk * OWN + OWN + 256
        seg = np.zeros((HALO, D), np.float32)
        s0, s1 = max(lo, 0), min(hi, S)
        seg[s0 - lo:s1 - lo] = xflat[seq * S + s0:seq * S + s1]
        segT = seg.T
        xp1 = np.ascontiguousarray(np.stack(
            [segT[:, 512 * t:512 * t + 1024].reshape(NCH, 128, 1024)
             for t in range(NT)]).reshape(-1, 1024))
        xp7 = np.ascontiguousarray(np.stack(
            [segT[:, 512 * t + 256:512 * t + 768].reshape(NCH, 128, 512)
             for t in range(NT)]).reshape(-1, 512))
        pos_q = chunk * OWN + np.arange(OWN)
        pos_k = np.clip(lo + np.arange(HALO), 0, S - 1)
        alpha = np.empty((1, 1024), np.float32)
        alpha[0, 0:512] = first if chunk == 0 else normal
        alpha[0, 512:1024] = normal
        mask = (np.zeros if chunk == 3 else np.ones)((128, 512), ml_dtypes.bfloat16)
        in_maps.append(dict(
            xp1=xp1, xp7=xp7,
            wqb=wqb, wkT=wkT, wvT=wvT, wob=wob, gb=gbb, ub=ubb, db=dbb,
            gsc=gsc, usc=usc, dsc=dsc,
            qcosT=np.ascontiguousarray(cos[pos_q].T).astype(ml_dtypes.bfloat16),
            qsinT=np.ascontiguousarray((sin[pos_q] * sgn).T).astype(ml_dtypes.bfloat16),
            kcosT=np.ascontiguousarray(cos[pos_k].T).astype(ml_dtypes.bfloat16),
            ksinT=np.ascontiguousarray((sin[pos_k] * sgn).T).astype(ml_dtypes.bfloat16),
            alpha=alpha, mask=mask, onesr=onr, onesc=onc,
        ))
    return in_maps


def kernel(**inputs):
    in_maps = make_inmaps(**inputs)
    if "nc" not in _NC_CACHE:
        _NC_CACHE["nc"] = build_nc()
    nc = _NC_CACHE["nc"]
    res = run_bass_kernel_spmd(nc, in_maps, core_ids=list(range(NCORES)))
    rows = []
    for r in res.results:
        blk = r["outT"].reshape(NT, NCH, 128, 512)
        full = blk.transpose(1, 2, 0, 3).reshape(D, OWN)
        rows.append(full.T)
    out = np.concatenate(rows, axis=0).reshape(B, S, D)
    return np.ascontiguousarray(out.astype(np.float32))


# version 15
# speedup vs baseline: 1.1009x; 1.1009x over previous
import numpy as np
import ml_dtypes
import concourse.bass as bass
import concourse.mybir as mybir
from concourse.bass_utils import run_bass_kernel_spmd
from contextlib import ExitStack

F32 = mybir.dt.float32
BF16 = mybir.dt.bfloat16
I32 = mybir.dt.int32
AF = mybir.ActivationFunctionType
ALU = mybir.AluOpType
f32 = np.float32

B, S, D = 2, 8192, 4096
H, DH = 32, 128
DFF = 11008
NCORES = 8
OWN = 2048
HALO = OWN + 512
T = 512
NT = OWN // T
NCH = D // 128
FFCH = DFF // 128
ATT_SCALE = 1.0 / float(np.sqrt(DH))
EPS = 1e-5
NSLAB = 6
NXG = 4

ENGINES = ("sync", "tensor", "scalar", "vector", "gpsimd")
SEMS = (["ws%d" % i for i in range(NSLAB)]
        + ["xg%d" % i for i in range(NXG)]
        + ["xs0", "xs1", "sg0", "sg1",
           "os0", "os1", "ad", "od", "pe", "ac", "dv", "gp", "wb"])


def build_nc(ntiles=NT, debug=False, lockstep=False):
    nc = bass.Bass("TRN2", target_bir_lowering=False)

    xp1 = nc.dram_tensor("xp1", [NT * NCH * 128, 1024], F32,
                         kind="ExternalInput").ap()
    xp7 = nc.dram_tensor("xp7", [NT * NCH * 128, 512], F32,
                         kind="ExternalInput").ap()
    wqb = nc.dram_tensor("wqb", [8 * NCH * 128, 512], F32,
                         kind="ExternalInput").ap()
    wob = nc.dram_tensor("wob", [8 * NCH * 128, 512], F32,
                         kind="ExternalInput").ap()
    wkT = nc.dram_tensor("wkT", [D, DH], F32, kind="ExternalInput").ap()
    wvT = nc.dram_tensor("wvT", [D, DH], F32, kind="ExternalInput").ap()
    gb = nc.dram_tensor("gb", [22 * NCH * 128, 512], I32,
                        kind="ExternalInput").ap()
    ub = nc.dram_tensor("ub", [22 * NCH * 128, 512], I32,
                        kind="ExternalInput").ap()
    db = nc.dram_tensor("db", [8 * FFCH * 128, 512], I32,
                        kind="ExternalInput").ap()
    gsc_d = nc.dram_tensor("gsc", [128, FFCH], F32, kind="ExternalInput").ap()
    usc_d = nc.dram_tensor("usc", [128, FFCH], F32, kind="ExternalInput").ap()
    dsc_d = nc.dram_tensor("dsc", [128, NCH], F32, kind="ExternalInput").ap()
    qcos_d = nc.dram_tensor("qcosT", [DH, OWN], BF16, kind="ExternalInput").ap()
    qsin_d = nc.dram_tensor("qsinT", [DH, OWN], BF16, kind="ExternalInput").ap()
    kcos_d = nc.dram_tensor("kcosT", [DH, HALO], BF16, kind="ExternalInput").ap()
    ksin_d = nc.dram_tensor("ksinT", [DH, HALO], BF16, kind="ExternalInput").ap()
    alpha_d = nc.dram_tensor("alpha", [1, 1024], F32, kind="ExternalInput").ap()
    mask_d = nc.dram_tensor("mask", [128, 512], BF16, kind="ExternalInput").ap()
    onr_d = nc.dram_tensor("onesr", [1, 128], F32, kind="ExternalInput").ap()
    onc_d = nc.dram_tensor("onesc", [128, 1], F32, kind="ExternalInput").ap()
    outT = nc.dram_tensor("outT", [NT * NCH * 128, 512], F32,
                          kind="ExternalOutput").ap()
    spk = "ExternalOutput" if debug else "Internal"
    y1sp = nc.dram_tensor("y1sp", [2 * NCH * 128, 512], F32, kind=spk).ap()
    dsp = nc.dram_tensor("dsp", [2 * NCH * 128, 512], F32, kind=spk).ap()
    wqc = nc.dram_tensor("wqc", [8 * NCH * 128, 512], BF16, kind="Internal").ap()
    woc = nc.dram_tensor("woc", [8 * NCH * 128, 512], BF16, kind="Internal").ap()
    dbg = {}
    if debug:
        for nm, sh, dt_ in (("dln1", [128, NCH * 1024], BF16),
                            ("dkr", [128, 1024], BF16), ("dvt", [128, 1024], BF16),
                            ("dqr", [128, NCH * 512], BF16),
                            ("dao", [128, NCH * 512], BF16),
                            ("dO", [128, NCH * 512], F32),
                            ("dln2", [128, NCH * 512], BF16),
                            ("dh1", [128, NCH * 1024], BF16),
                            ("dh2", [128, NCH * 512], BF16)):
            dbg[nm] = nc.dram_tensor(nm, sh, dt_, kind="ExternalOutput").ap()

    es = ExitStack()
    sb = lambda n, sh, dt: es.enter_context(nc.sbuf_tensor(n, sh, dt))
    with es:
        A = sb("A", [128, NCH * 1024], BF16)
        A32 = A.bitcast(F32)
        Bq = sb("Bq", [128, NCH * 512], BF16)
        Cao = sb("Cao", [128, NCH * 512], BF16)
        KR = sb("KR", [128, 1024], BF16)
        VT = sb("VT", [128, 1024], BF16)
        SS = sb("SS", [128, 2048], BF16)
        QCo = sb("QCo", [128, 512], BF16)
        QSi = sb("QSi", [128, 512], BF16)
        KCo = sb("KCo", [128, 1024], BF16)
        KSi = sb("KSi", [128, 1024], BF16)
        STG = [sb(f"STG{i}", [128, 512], F32) for i in range(NSLAB)]
        STGI = [t.bitcast(I32) for t in STG]
        STB = [sb(f"STB{i}", [128, 512], BF16) for i in range(NSLAB)]
        XST = [sb(f"XST{i}", [128, 1024], F32) for i in range(2)]
        SQ = [sb(f"SQ{i}", [128, 1024], BF16) for i in range(2)]
        OBS = [SQ[i][:, 512:1024] for i in range(2)]
        SG = sb("SG", [128, 2048], BF16)
        SG32 = SG.bitcast(F32)
        XG = [sb(f"XG{i}", [128, 512], F32) for i in range(NXG)]
        RPS = [sb("RPS0", [128, 512], F32)]
        T12 = sb("T12", [128, 512], F32)
        MR = sb("MR", [1, 1024], F32)
        RR = sb("RR", [1, 1024], F32)
        ARW = sb("ARW", [1, 1024], F32)
        ALP = sb("ALP", [1, 1024], F32)
        MSK = sb("MSK", [128, 512], BF16)
        GSC = sb("GSCs", [128, FFCH], F32)
        USC = sb("USCs", [128, FFCH], F32)
        DSC = sb("DSCs", [128, NCH], F32)
        ONR = sb("ONRs", [1, 128], F32)
        ONC = sb("ONCs", [128, 1], F32)
        ONCB = sb("ONCBs", [128, 1], BF16)
        pb = [es.enter_context(nc.psum_tensor(f"pb{i}", [128, 512], F32))
              for i in range(8)]
        sems = {s: es.enter_context(nc.semaphore(s)) for s in SEMS}

        cnt = {s: 0 for s in SEMS}
        hw = {e: {s: 0 for s in SEMS} for e in ENGINES}
        q = {e: [] for e in ENGINES}

        def op(eng, fn, waits=(), inc=None, inc_by=1):
            if lockstep:
                waits = list(waits) + [(s, cnt[s]) for s in SEMS]
            ws = []
            for s, v in waits:
                if v > 0 and v > hw[eng][s]:
                    hw[eng][s] = v
                    ws.append((sems[s], v))
            tgt = None
            if inc is not None:
                cnt[inc] += inc_by
                tgt = (sems[inc], inc_by)

            def emit(e, fn=fn, ws=ws, tgt=tgt):
                for h_, v in ws:
                    e.wait_ge(h_, v)
                r = fn(e)
                if tgt is not None:
                    r.then_inc(*tgt)
            q[eng].append(emit)

        def barrier():
            snap = [(s, cnt[s]) for s in ("pe", "ac", "dv", "ad", "gp")]
            for e in ("tensor", "scalar", "vector", "gpsimd"):
                for s, v in snap:
                    if v > hw[e][s]:
                        hw[e][s] = v

                        def emit(eng_, h_=sems[s], v_=v):
                            eng_.wait_ge(h_, v_)
                        q[e].append(emit)

        ring = {"seq": 0, "uses": [0] * NSLAB,
                "stage_free": [("ac", 0)] * NSLAB,
                "wb_free": [("wb", 0)] * NSLAB,
                "bf_free": [0] * NSLAB}

        def slab(src_ap, cols, caster, is_int=False, slot=None):
            sl = ring["seq"] % NSLAB if slot is None else slot
            if slot is None:
                ring["seq"] += 1
            ring["uses"][sl] += 1
            wsem = "ws%d" % sl
            cs, cv = ring["stage_free"][sl]

            def dma_fn(e, sl=sl, c=cols, ii=is_int, s=src_ap):
                return e.dma_start(STGI[sl][:, 0:c] if ii else STG[sl][:, 0:c], s)
            op("sync", dma_fn, waits=[(cs, cv)], inc=wsem, inc_by=16)
            wd_v = cnt[wsem]
            csem = "dv" if caster == "vector" else "ac"

            if caster == "vector":
                def cast_fn(e, sl=sl, c=cols, ii=is_int):
                    return e.tensor_copy(STB[sl][:, 0:c],
                                         STGI[sl][:, 0:c] if ii else STG[sl][:, 0:c])
            else:
                def cast_fn(e, sl=sl, c=cols, ii=is_int):
                    return e.activation(STB[sl][:, 0:c],
                                        STGI[sl][:, 0:c] if ii else STG[sl][:, 0:c],
                                        AF.Copy, scale=f32(1.0))
            op(caster, cast_fn, waits=[(wsem, wd_v), ("pe", ring["bf_free"][sl]),
                                       ring["wb_free"][sl]],
               inc=csem)
            ring["stage_free"][sl] = (csem, cnt[csem])
            return sl, (csem, cnt[csem])

        def slab_done(sl):
            ring["bf_free"][sl] = cnt["pe"]

        def slab_direct(src_ap, cols, extra=()):
            sl = ring["seq"] % NSLAB
            ring["seq"] += 1
            wsem = "ws%d" % sl
            op("sync", lambda e, sl=sl, c=cols, s=src_ap:
                e.dma_start(STB[sl][:, 0:c], s),
               waits=[("pe", ring["bf_free"][sl]), ring["wb_free"][sl]]
                     + list(extra),
               inc=wsem, inc_by=16)
            return sl, (wsem, cnt[wsem])

        def dump(nm, src_ap):
            if not debug:
                return
            snap = [(s, cnt[s]) for s in ("pe", "ac", "dv", "ad", "gp")]
            op("gpsimd", lambda e, d=dbg[nm], s=src_ap: e.dma_start(d, s),
               waits=snap, inc="od", inc_by=16)
            odv = cnt["od"]
            for e in ("tensor", "scalar", "vector", "gpsimd"):
                hw[e]["od"] = odv

                def emit(eng_, h_=sems["od"], v_=odv):
                    eng_.wait_ge(h_, v_)
                q[e].append(emit)

        xst_free = [[("ac", 0)], [("ac", 0)]]

        def xdma(dst_slice, src_ap, slot, extra=()):
            ws = list(xst_free[slot]) + list(extra)
            op("scalar", lambda e, d=dst_slice, sa=src_ap: e.dma_start(d, sa),
               waits=ws, inc="xs%d" % slot, inc_by=16)
            return cnt["xs%d" % slot]

        xg = {"seq": 0, "free": [("dv", 0)] * NXG}

        def gdma(src_ap, extra=()):
            sl = xg["seq"] % NXG
            xg["seq"] += 1
            ws = [xg["free"][sl]] + list(extra)
            op("sync", lambda e, sl=sl, sa=src_ap: e.dma_start(XG[sl][:, :], sa),
               waits=ws, inc="xg%d" % sl, inc_by=16)
            return sl, cnt["xg%d" % sl]

        def ln_smalls(W, s_banks, q_banks, m_banks, r_banks, bank_free=()):
            nh = W // 512
            pe_stats = cnt["pe"]
            for h in range(nh):
                op("scalar", lambda e, h=h, b=s_banks[h]:
                    e.activation(MR[0:1, 512 * h:512 * h + 512], b[0:1, :], AF.Copy,
                                 scale=f32(1.0 / D)),
                   waits=[("pe", pe_stats)])
            for h in range(nh):
                op("scalar", lambda e, h=h:
                    e.activation(RR[0:1, 512 * h:512 * h + 512],
                                 MR[0:1, 512 * h:512 * h + 512],
                                 AF.Square, scale=f32(1.0)),
                   inc="ac" if h == nh - 1 else None)
            ac_m = cnt["ac"]
            for h in range(nh):
                op("tensor", lambda e, h=h, b=m_banks[h]:
                    e.matmul(b[:, :], ONR[0:1, :], MR[0:1, 512 * h:512 * h + 512],
                             start=True, stop=True),
                   waits=[("ac", ac_m)] + list(bank_free),
                   inc="pe" if h == nh - 1 else None)
            pe_mb = cnt["pe"]
            for h in range(nh):
                op("vector", lambda e, h=h, b=q_banks[h]:
                    e.scalar_tensor_tensor(MR[0:1, 512 * h:512 * h + 512], b[0:1, :],
                                           float(1.0 / D), RR[0:1, 512 * h:512 * h + 512],
                                           op0=ALU.mult, op1=ALU.subtract),
                   waits=[("pe", pe_mb), ("ac", ac_m)],
                   inc="dv" if h == nh - 1 else None)
            dv_var = cnt["dv"]
            for h in range(nh):
                op("scalar", lambda e, h=h:
                    e.activation(RR[0:1, 512 * h:512 * h + 512],
                                 MR[0:1, 512 * h:512 * h + 512],
                                 AF.Sqrt, bias=f32(EPS), scale=f32(1.0)),
                   waits=[("dv", dv_var)], inc="ac" if h == nh - 1 else None)
            ac_sq = cnt["ac"]
            op("vector", lambda e, W=W: e.reciprocal(MR[0:1, 0:W], RR[0:1, 0:W]),
               waits=[("ac", ac_sq)], inc="dv")
            dv_r = cnt["dv"]
            for h in range(nh):
                op("tensor", lambda e, h=h, b=r_banks[h]:
                    e.matmul(b[:, :], ONR[0:1, :], MR[0:1, 512 * h:512 * h + 512],
                             start=True, stop=True),
                   waits=[("dv", dv_r)], inc="pe" if h == nh - 1 else None)
            return cnt["pe"]

        for dst, src in ((GSC, gsc_d), (USC, usc_d), (DSC, dsc_d), (ALP, alpha_d),
                         (MSK, mask_d), (ONR, onr_d), (ONC, onc_d)):
            op("scalar", lambda e, d=dst, s=src: e.dma_start(d[:], s),
               inc="ad", inc_by=16)
        ad_init = cnt["ad"]
        op("vector", lambda e: e.tensor_copy(ONCB[:], ONC[:]), waits=[("ad", ad_init)],
           inc="dv")

        out_free = [[("os0", 0)], [("os1", 0)]]
        wbst = {"wq": 0, "wo": 0}
        p5st = {"set_sig": [("ac", 0)] * 3, "set_rs": [("dv", 0)] * 3,
                "set_bc": [("dv", 0)] * 3, "ss_use": [0, 0],
                "av_free": [0, 0], "arw_free": [0, 0], "msk_free": 0,
                "qk_pe": {}, "sig_ac": {}, "sig_dv": {}, "rs_pe": {},
                "av_pe": {}, "arw_dv": {}}

        for t in range(ntiles):
            h0 = 512 * t
            rope_dv_prev = cnt["dv"]
            pe_tile0 = cnt["pe"]
            pe_prev = wbst.get("pe_p10", 0) or pe_tile0
            dv_tile0 = cnt["dv"]
            ac_tile0 = cnt["ac"]

            op("scalar", lambda e, t=t: e.dma_start(QCo[:], qcos_d[:, T * t:T * t + T]),
               waits=[("dv", rope_dv_prev)], inc="ad", inc_by=16)
            op("scalar", lambda e, t=t: e.dma_start(QSi[:], qsin_d[:, T * t:T * t + T]),
               inc="ad", inc_by=16)
            op("scalar", lambda e, h0=h0: e.dma_start(KCo[:], kcos_d[:, h0:h0 + 1024]),
               inc="ad", inc_by=16)
            op("scalar", lambda e, h0=h0: e.dma_start(KSi[:], ksin_d[:, h0:h0 + 1024]),
               inc="ad", inc_by=16)
            ad_rope = cnt["ad"]

            sq_free = [pe_prev, pe_prev]
            for i in range(NCH):
                ad_v = xdma(XST[i % 2][:],
                            xp1[128 * (NCH * t + i):128 * (NCH * t + i) + 128, :],
                            i % 2)
                op("scalar", lambda e, i=i: e.activation(A[:, 1024 * i:1024 * i + 1024],
                                                         XST[i % 2][:], AF.Copy,
                                                         scale=f32(1.0)),
                   waits=[("xs%d" % (i % 2), ad_v), ("pe", sq_free[i % 2]),
                          ("pe", pe_prev)], inc="ac")
                ac_v = cnt["ac"]
                op("scalar", lambda e, i=i: e.activation(SQ[i % 2][:], XST[i % 2][:],
                                                         AF.Square, scale=f32(1.0)),
                   inc="gp")
                gp_v = cnt["gp"]
                xst_free[i % 2] = [("ac", ac_v), ("gp", gp_v)]
                for hh in range(2):
                    op("tensor", lambda e, i=i, hh=hh:
                        e.matmul(pb[hh][0:1, :], ONCB[:],
                                 A[:, 1024 * i + 512 * hh:1024 * i + 512 * hh + 512],
                                 start=(i == 0), stop=(i == NCH - 1)),
                       waits=[("ac", ac_v)])
                for hh in range(2):
                    op("tensor", lambda e, i=i, hh=hh:
                        e.matmul(pb[2 + hh][0:1, :], ONCB[:],
                                 SQ[i % 2][:, 512 * hh:512 * hh + 512],
                                 start=(i == 0), stop=(i == NCH - 1)),
                       waits=[("gp", gp_v)], inc="pe" if hh == 1 else None)
                sq_free[i % 2] = cnt["pe"]

            pe_bc1 = ln_smalls(1024, [pb[0], pb[1]], [pb[2], pb[3]],
                               [pb[4], pb[5]], [pb[6], pb[7]],
                               bank_free=[("dv", dv_tile0), ("ac", ac_tile0)])
            for i in range(NCH):
                for hh in range(2):
                    op("vector", lambda e, i=i, hh=hh:
                        e.tensor_tensor(XST[0][:, 512 * hh:512 * hh + 512],
                                        A[:, 1024 * i + 512 * hh:1024 * i + 512 * hh + 512],
                                        pb[4 + hh][:, :], op=ALU.subtract),
                       waits=[("pe", pe_bc1)])
                for hh in range(2):
                    op("vector", lambda e, i=i, hh=hh:
                        e.tensor_tensor(A[:, 1024 * i + 512 * hh:1024 * i + 512 * hh + 512],
                                        XST[0][:, 512 * hh:512 * hh + 512],
                                        pb[6 + hh][:, :], op=ALU.mult),
                       inc="dv" if hh == 1 else None)
            dv_ln1 = cnt["dv"]
            xst_free[0] = [("dv", dv_ln1)]
            barrier()
            if t == 0:
                dump("dln1", A[:])

            ac_v0 = 0
            for vpass in range(2):
                for i in range(NCH):
                    if vpass == 0:
                        slk, rk = slab(wkT[128 * i:128 * i + 128, :], 128, "scalar")
                    slv, rv = slab(wvT[128 * i:128 * i + 128, :], 128, "scalar")
                    if vpass == 0:
                        for hh in range(2):
                            op("tensor", lambda e, i=i, hh=hh, sl=slk:
                                e.matmul(pb[hh][:, :], STB[sl][:, 0:128],
                                         A[:, 1024 * i + 512 * hh:1024 * i + 512 * hh + 512],
                                         start=(i == 0), stop=(i == NCH - 1)),
                               waits=[rk, ("dv", dv_ln1)],
                               inc="pe" if hh == 1 else None)
                        slab_done(slk)
                    for kc in range(4):
                        ck = 4 * vpass + kc
                        op("tensor", lambda e, i=i, kc=kc, ck=ck, sl=slv:
                            e.matmul(pb[2 + kc][:, 0:128],
                                     A[:, 1024 * i + 128 * ck:1024 * i + 128 * ck + 128],
                                     STB[sl][:, 0:128],
                                     start=(i == 0), stop=(i == NCH - 1)),
                           waits=[rv, ("dv", dv_ln1), ("ac", ac_v0)],
                           inc="pe" if kc == 3 else None)
                    slab_done(slv)
                pe_v = cnt["pe"]
                for kc in range(4):
                    ck = 4 * vpass + kc
                    op("scalar", lambda e, kc=kc, ck=ck:
                        e.activation(VT[:, 128 * ck:128 * ck + 128], pb[2 + kc][:, 0:128],
                                     AF.Copy, scale=f32(1.0)),
                       waits=[("pe", pe_v)], inc="ac" if kc == 3 else None)
                ac_v0 = cnt["ac"]
                if vpass == 0:
                    pe_k = pe_v
                    for hh in range(2):
                        c0 = 512 * hh
                        op("scalar", lambda e, hh=hh, c0=c0: e.activation(
                            XST[1][0:64, c0:c0 + 512], pb[hh][64:128, :], AF.Copy,
                            scale=f32(1.0)),
                           waits=[("pe", pe_k)])
                        op("scalar", lambda e, hh=hh, c0=c0: e.activation(
                            XST[1][64:128, c0:c0 + 512], pb[hh][0:64, :], AF.Copy,
                            scale=f32(1.0)),
                           inc="ac")
                        ac_rot = cnt["ac"]
                        op("vector", lambda e, hh=hh, c0=c0: e.tensor_tensor(
                            XST[0][:, c0:c0 + 512], XST[1][:, c0:c0 + 512],
                            KSi[:, c0:c0 + 512], op=ALU.mult),
                           waits=[("ac", ac_rot), ("ad", ad_rope)])
                        op("vector", lambda e, hh=hh, c0=c0: e.tensor_tensor(
                            XST[1][:, c0:c0 + 512], pb[hh][:, :],
                            KCo[:, c0:c0 + 512], op=ALU.mult))
                        op("vector", lambda e, hh=hh, c0=c0: e.tensor_tensor(
                            KR[:, c0:c0 + 512], XST[0][:, c0:c0 + 512],
                            XST[1][:, c0:c0 + 512], op=ALU.add),
                           inc="dv")
                    xst_free[0] = [("dv", cnt["dv"])]
                    xst_free[1] = [("dv", cnt["dv"])]
            barrier()
            if t == 0:
                dump("dkr", KR[:])
                dump("dvt", VT[:])

            qrope_free = [0, 0]
            for og in range(8):
                pbase = (og % 2) * 4
                for i in range(NCH):
                    bi = 128 * (NCH * og + i)
                    if t == 0:
                        sl, r = slab(wqb[bi:bi + 128, :], 512, "scalar")
                        op("scalar", lambda e, sl=sl, bi=bi: e.dma_start(
                            wqc[bi:bi + 128, :], STB[sl][:, 0:512]),
                           waits=[r], inc="wb", inc_by=16)
                        ring["wb_free"][sl] = ("wb", cnt["wb"])
                    else:
                        sl, r = slab_direct(
                            wqc[bi:bi + 128, :], 512,
                            extra=([("wb", wbst["wq"])] if t == 1 else ()))
                    for j in range(4):
                        op("tensor", lambda e, i=i, j=j, sl=sl, pbase=pbase:
                            e.matmul(pb[pbase + j][:, :], STB[sl][:, 128 * j:128 * j + 128],
                                     A[:, 1024 * i + 256:1024 * i + 768],
                                     start=(i == 0), stop=(i == NCH - 1)),
                           waits=[r, ("dv", qrope_free[og % 2])],
                           inc="pe" if j == 3 else None)
                    slab_done(sl)
                pe_q = cnt["pe"]
                for j in range(4):
                    hq = 4 * og + j
                    bk = pb[pbase + j]
                    op("scalar", lambda e, bk=bk: e.activation(
                        XST[0][0:64, 0:512], bk[64:128, :], AF.Copy, scale=f32(1.0)),
                       waits=[("pe", pe_q), ("dv", cnt["dv"])])
                    op("scalar", lambda e, bk=bk: e.activation(
                        XST[0][64:128, 0:512], bk[0:64, :], AF.Copy, scale=f32(1.0)),
                       inc="ac")
                    ac_rot = cnt["ac"]
                    op("vector", lambda e: e.tensor_tensor(
                        XST[0][:, 512:1024], XST[0][:, 0:512], QSi[:, :], op=ALU.mult),
                       waits=[("ac", ac_rot), ("ad", ad_rope)])
                    op("vector", lambda e, bk=bk: e.tensor_tensor(
                        XST[0][:, 0:512], bk[:, :], QCo[:, :], op=ALU.mult))
                    op("vector", lambda e, hq=hq: e.tensor_tensor(
                        Bq[:, 512 * hq:512 * hq + 512], XST[0][:, 512:1024],
                        XST[0][:, 0:512], op=ALU.add),
                       inc="dv")
                qrope_free[og % 2] = cnt["dv"]
            xst_free[0] = [("dv", cnt["dv"])]
            if t == 0:
                wbst["wq"] = cnt["wb"]
            barrier()
            if t == 0:
                dump("dqr", Bq[:])

            SCB = ((0, 1), (2, 3), (4, 5))

            def p5_qk(s):
                bh, w = s // 2, s % 2
                b, hq = bh % 2, bh // 2
                qc0 = 512 * hq + 256 * b
                kb = 256 * b + 256 * w
                st = s % 3
                ws = [p5st["set_sig"][st], p5st["set_rs"][st], p5st["set_bc"][st]]
                for kc in range(4):
                    bank = pb[SCB[st][kc // 2]]
                    cc = 256 * (kc % 2)
                    op("tensor", lambda e, bank=bank, cc=cc, kb=kb, kc=kc, qc0=qc0:
                        e.matmul(bank[:, cc:cc + 256],
                                 KR[:, kb + 128 * kc:kb + 128 * kc + 128],
                                 Bq[:, qc0:qc0 + 256], start=True, stop=True),
                       waits=ws if kc == 0 else (),
                       inc="pe" if kc == 3 else None)
                p5st["qk_pe"][s] = cnt["pe"]

            def p5_sig(s):
                bh, w = s // 2, s % 2
                b = bh % 2
                st = s % 3
                masked = (t == NT - 1 and b == 1 and w == 1)
                base = 1024 * (s % 2)
                ws = [("pe", p5st["qk_pe"][s]), ("pe", p5st["ss_use"][s % 2])]
                op("scalar", lambda e, st=st, base=base:
                    e.activation(SS[:, base:base + 512], pb[SCB[st][0]][:, :],
                                 AF.Sigmoid, scale=f32(ATT_SCALE)),
                   waits=ws)
                if masked:
                    op("scalar", lambda e, st=st:
                        e.activation(SQ[0][:, 0:512], pb[SCB[st][1]][:, :],
                                     AF.Sigmoid, scale=f32(ATT_SCALE)),
                       waits=[("dv", p5st["msk_free"])], inc="ac")
                    op("vector", lambda e, base=base:
                        e.tensor_tensor(SS[:, base + 512:base + 1024],
                                        SQ[0][:, 0:512], MSK[:, :], op=ALU.mult),
                       waits=[("ac", cnt["ac"])], inc="dv")
                    p5st["msk_free"] = cnt["dv"]
                    p5st["sig_dv"][s] = cnt["dv"]
                else:
                    op("scalar", lambda e, st=st, base=base:
                        e.activation(SS[:, base + 512:base + 1024],
                                     pb[SCB[st][1]][:, :], AF.Sigmoid,
                                     scale=f32(ATT_SCALE)),
                       inc="ac")
                    p5st["sig_dv"][s] = 0
                p5st["sig_ac"][s] = cnt["ac"]
                p5st["set_sig"][st] = ("ac", cnt["ac"])

            def p5_rs_av(s):
                bh, w = s // 2, s % 2
                b = bh % 2
                kb = 256 * b + 256 * w
                st = s % 3
                base = 1024 * (s % 2)
                ws = [("ac", p5st["sig_ac"][s])]
                if p5st["sig_dv"][s]:
                    ws.append(("dv", p5st["sig_dv"][s]))
                rsb = pb[SCB[st][0]]
                for kc in range(4):
                    op("tensor", lambda e, rsb=rsb, base=base, kc=kc:
                        e.matmul(rsb[0:1, 0:256], ONCB[:],
                                 SS[:, base + 256 * kc:base + 256 * kc + 256],
                                 start=(kc == 0), stop=(kc == 3)),
                       waits=ws if kc == 0 else (),
                       inc="pe" if kc == 3 else None)
                p5st["rs_pe"][s] = cnt["pe"]
                avb = pb[6 + (s // 2) % 2]
                avc = 256 * (s % 2)
                avw = [("dv", p5st["av_free"][(s // 2) % 2])]
                for kc in range(4):
                    ck = kb // 128 + kc
                    op("tensor", lambda e, avb=avb, avc=avc, ck=ck, base=base, kc=kc:
                        e.matmul(avb[:, avc:avc + 256],
                                 VT[:, 128 * ck:128 * ck + 128],
                                 SS[:, base + 256 * kc:base + 256 * kc + 256],
                                 start=(kc == 0), stop=(kc == 3)),
                       waits=avw if kc == 0 else (),
                       inc="pe" if kc == 3 else None)
                p5st["av_pe"][s] = cnt["pe"]
                p5st["ss_use"][s % 2] = cnt["pe"]

            def p5_vec_post(s):
                bh, w = s // 2, s % 2
                b = bh % 2
                st = s % 3
                k = s // 2
                arwc = 512 * (k % 2) + 256 * w
                sel = (0 if (t == 0 and b == 0) else 512) + 256 * w
                rsb = pb[SCB[st][0]]
                rrc = 256 * (s % 4)
                op("vector", lambda e, rsb=rsb, rrc=rrc:
                    e.reciprocal(RR[0:1, rrc:rrc + 256], rsb[0:1, 0:256]),
                   waits=[("pe", p5st["rs_pe"][s])])
                op("vector", lambda e, arwc=arwc, sel=sel, rrc=rrc:
                    e.tensor_tensor(ARW[0:1, arwc:arwc + 256],
                                    RR[0:1, rrc:rrc + 256],
                                    ALP[0:1, sel:sel + 256], op=ALU.mult),
                   waits=[("pe", p5st["arw_free"][k % 2]), ("ad", ad_init)],
                   inc="dv")
                p5st["set_rs"][st] = ("dv", cnt["dv"])
                p5st["arw_dv"][s] = cnt["dv"]

            def p5_bc_blend(k):
                st = (2 * k + 2) % 3
                bcb = pb[SCB[st][1]]
                arw0 = 512 * (k % 2)
                op("tensor", lambda e, bcb=bcb, arw0=arw0:
                    e.matmul(bcb[:, :], ONR[0:1, :], ARW[0:1, arw0:arw0 + 512],
                             start=True, stop=True),
                   waits=[("dv", p5st["arw_dv"][2 * k + 1]),
                          p5st["set_sig"][st], p5st["set_bc"][st]],
                   inc="pe")
                p5st["arw_free"][k % 2] = cnt["pe"]
                bc_pe = cnt["pe"]
                op("vector", lambda e, bcb=bcb: e.tensor_copy(RPS[0][:, :], bcb[:, :]),
                   waits=[("pe", bc_pe)])
                avb = pb[6 + (k % 2)]
                op("vector", lambda e, avb=avb: e.tensor_tensor(
                    T12[:, 0:256], avb[:, 0:256], RPS[0][:, 0:256], op=ALU.mult),
                   waits=[("pe", p5st["av_pe"][2 * k])])
                op("vector", lambda e, avb=avb: e.tensor_tensor(
                    T12[:, 256:512], avb[:, 256:512], RPS[0][:, 256:512], op=ALU.mult),
                   waits=[("pe", p5st["av_pe"][2 * k + 1])])
                b, hq = k % 2, k // 2
                op("vector", lambda e, hq=hq, b=b: e.tensor_tensor(
                    Cao[:, 512 * hq + 256 * b:512 * hq + 256 * b + 256],
                    T12[:, 0:256], T12[:, 256:512], op=ALU.add),
                   inc="dv")
                p5st["av_free"][k % 2] = cnt["dv"]
                p5st["set_bc"][st] = ("dv", cnt["dv"])

            NSLOT = 128
            for s in range(NSLOT + 2):
                if s < NSLOT:
                    p5_qk(s)
                    p5_sig(s)
                if 1 <= s <= NSLOT:
                    p5_rs_av(s - 1)
                    p5_vec_post(s - 1)
                if s >= 3 and s % 2 == 1:
                    p5_bc_blend((s - 3) // 2)
            barrier()
            if t == 0:
                dump("dao", Cao[:])

            SETS = ((0, 1), (2, 3), (6, 7))
            set_free = [("ac", 0)] * 3
            obs_free = [0, 0]
            for og in range(8):
                sA, sB = SETS[(2 * og) % 3], SETS[(2 * og + 1) % 3]
                bkof = [sA[0], sA[1], sB[0], sB[1]]
                for i in range(NCH):
                    bi = 128 * (NCH * og + i)
                    if t == 0:
                        sl, r = slab(wob[bi:bi + 128, :], 512, "scalar")
                        op("scalar", lambda e, sl=sl, bi=bi: e.dma_start(
                            woc[bi:bi + 128, :], STB[sl][:, 0:512]),
                           waits=[r], inc="wb", inc_by=16)
                        ring["wb_free"][sl] = ("wb", cnt["wb"])
                    else:
                        sl, r = slab_direct(
                            woc[bi:bi + 128, :], 512,
                            extra=([("wb", wbst["wo"])] if t == 1 else ()))
                    for j in range(4):
                        op("tensor", lambda e, i=i, j=j, sl=sl, bk=bkof[j]:
                            e.matmul(pb[bk][:, :], STB[sl][:, 128 * j:128 * j + 128],
                                     Cao[:, 512 * i:512 * i + 512],
                                     start=(i == 0), stop=(i == NCH - 1)),
                           waits=[r, set_free[(2 * og + j // 2) % 3]],
                           inc="pe" if j == 3 else None)
                    slab_done(sl)
                pe_wo = cnt["pe"]
                for j in range(4):
                    o = 4 * og + j
                    op("scalar", lambda e, j=j, o=o, bk=bkof[j]: e.activation(
                        A32[:, 512 * o:512 * o + 512], pb[bk][:, :], AF.Copy,
                        scale=f32(1.0)),
                       waits=[("pe", pe_wo)], inc="ac")
                    ac_e = cnt["ac"]
                    if j == 1:
                        set_free[(2 * og) % 3] = ("ac", ac_e)
                    if j == 3:
                        set_free[(2 * og + 1) % 3] = ("ac", ac_e)
                    op("vector", lambda e, o=o: e.tensor_copy(
                        OBS[o % 2][:, :], A32[:, 512 * o:512 * o + 512]),
                       waits=[("ac", ac_e), ("pe", obs_free[o % 2])])
                    op("vector", lambda e, o=o: e.tensor_tensor(
                        SQ[o % 2][:, 0:512], A32[:, 512 * o:512 * o + 512],
                        A32[:, 512 * o:512 * o + 512], op=ALU.mult),
                       inc="dv")
                    dv_e = cnt["dv"]
                    op("tensor", lambda e, o=o: e.matmul(pb[4][0:1, :], ONCB[:],
                                                         OBS[o % 2][:, :],
                                                         start=(o == 0), stop=(o == NCH - 1)),
                       waits=[("dv", dv_e)])
                    op("tensor", lambda e, o=o: e.matmul(pb[5][0:1, :], ONCB[:],
                                                         SQ[o % 2][:, 0:512],
                                                         start=(o == 0), stop=(o == NCH - 1)),
                       inc="pe")
                    obs_free[o % 2] = cnt["pe"]
            if t == 0:
                wbst["wo"] = cnt["wb"]
                dump("dO", A32[:])
            pe_bcO = ln_smalls(512, [pb[4]], [pb[5]], [pb[6]], [pb[7]],
                               bank_free=[set_free[2]])

            sg_free = [list(out_free[0]), list(out_free[1])]
            y1_evt = []
            for o in range(NCH):
                sl, gv = gdma(xp7[128 * (NCH * t + o):128 * (NCH * t + o) + 128, :])
                sgs = o % 2
                op("vector", lambda e, o=o: e.tensor_tensor(
                    T12[:, :], A32[:, 512 * o:512 * o + 512], pb[6][:, :],
                    op=ALU.subtract),
                   waits=[("xg%d" % sl, gv), ("pe", pe_bcO)] + sg_free[sgs])
                op("vector", lambda e: e.tensor_tensor(
                    T12[:, :], T12[:, :], pb[7][:, :], op=ALU.mult))
                op("vector", lambda e, o=o, sgs=sgs, sl=sl: e.tensor_tensor(
                    SG32[:, 512 * sgs:512 * sgs + 512], T12[:, :], XG[sl][:, :],
                    op=ALU.add),
                   inc="dv")
                dv_y1 = cnt["dv"]
                xg["free"][sl] = ("dv", dv_y1)
                op("scalar", lambda e, sgs=sgs: e.activation(
                    OBS[sgs][:, :], SG32[:, 512 * sgs:512 * sgs + 512], AF.Copy,
                    scale=f32(1.0)),
                   waits=[("dv", dv_y1)])
                op("scalar", lambda e, sgs=sgs: e.activation(
                    SQ[sgs][:, 0:512], SG32[:, 512 * sgs:512 * sgs + 512], AF.Square,
                    scale=f32(1.0)),
                   inc="ac")
                ac_y = cnt["ac"]
                op("tensor", lambda e, o=o, sgs=sgs: e.matmul(
                    pb[0][0:1, :], ONCB[:], OBS[sgs][:, :],
                    start=(o == 0), stop=(o == NCH - 1)),
                   waits=[("ac", ac_y)])
                op("tensor", lambda e, o=o, sgs=sgs: e.matmul(
                    pb[1][0:1, :], ONCB[:], SQ[sgs][:, 0:512],
                    start=(o == 0), stop=(o == NCH - 1)),
                   inc="pe")
                op("scalar", lambda e, o=o, t=t, sgs=sgs: e.dma_start(
                    y1sp[128 * (NCH * (t % 2) + o):128 * (NCH * (t % 2) + o) + 128, :],
                    SG32[:, 512 * sgs:512 * sgs + 512]),
                   waits=[("dv", dv_y1)], inc="sg%d" % sgs, inc_by=16)
                y1_evt.append(("sg%d" % sgs, cnt["sg%d" % sgs]))
                sg_free[sgs] = [("sg%d" % sgs, cnt["sg%d" % sgs]), ("ac", ac_y)]
            y1_all = [("sg0", cnt["sg0"]), ("sg1", cnt["sg1"])]
            pe_bc2 = ln_smalls(512, [pb[0]], [pb[1]], [pb[2]], [pb[3]])

            for i in range(NCH):
                sl, gv = gdma(y1sp[128 * (NCH * (t % 2) + i):
                                   128 * (NCH * (t % 2) + i) + 128, :],
                              extra=y1_all)
                op("vector", lambda e, sl=sl: e.tensor_tensor(
                    T12[:, :], XG[sl][:, :], pb[2][:, :], op=ALU.subtract),
                   waits=[("xg%d" % sl, gv), ("pe", pe_bc2)])
                op("vector", lambda e, i=i: e.tensor_tensor(
                    Bq[:, 512 * i:512 * i + 512], T12[:, :], pb[3][:, :], op=ALU.mult),
                   inc="dv")
                xg["free"][sl] = ("dv", cnt["dv"])
            barrier()
            if t == 0:
                dump("dln2", Bq[:])

            silu_free = [("dv", 0)] * 4
            up_free = [("dv", 0)] * 4
            gate_free = [("ac", 0)] * 4
            for g in range(22):
                gw = min(512, DFF - 512 * g)
                nch_g = gw // 128
                for i in range(NCH):
                    bi = 128 * (NCH * g + i)
                    sl, r = slab(gb[bi:bi + 128, :], 512, "vector", is_int=True)
                    for j in range(nch_g):
                        op("tensor", lambda e, i=i, j=j, sl=sl:
                            e.matmul(pb[j][:, :], STB[sl][:, 128 * j:128 * j + 128],
                                     Bq[:, 512 * i:512 * i + 512],
                                     start=(i == 0), stop=(i == NCH - 1)),
                           waits=[r, gate_free[j]],
                           inc="pe" if j == nch_g - 1 else None)
                    slab_done(sl)
                pe_g = cnt["pe"]
                for j in range(nch_g):
                    c = 4 * g + j
                    op("scalar", lambda e, j=j, c=c: e.activation(
                        SG[:, 512 * j:512 * j + 512], pb[j][:, :], AF.Silu,
                        scale=GSC[:, c:c + 1]),
                       waits=[("pe", pe_g), silu_free[j]], inc="ac")
                    gate_free[j] = ("ac", cnt["ac"])
                ac_silu = cnt["ac"]
                for i in range(NCH):
                    bi = 128 * (NCH * g + i)
                    sl, r = slab(ub[bi:bi + 128, :], 512, "scalar", is_int=True)
                    for j in range(nch_g):
                        op("tensor", lambda e, i=i, j=j, sl=sl:
                            e.matmul(pb[4 + j][:, :], STB[sl][:, 128 * j:128 * j + 128],
                                     Bq[:, 512 * i:512 * i + 512],
                                     start=(i == 0), stop=(i == NCH - 1)),
                           waits=[r, up_free[j]],
                           inc="pe" if j == nch_g - 1 else None)
                    slab_done(sl)
                pe_u = cnt["pe"]
                for j in range(nch_g):
                    c = 4 * g + j
                    dst = (A, 512 * c) if c < 64 else (Cao, 512 * (c - 64))
                    op("vector", lambda e, j=j, c=c, dst=dst: e.scalar_tensor_tensor(
                        dst[0][:, dst[1]:dst[1] + 512], pb[4 + j][:, :], USC[:, c:c + 1],
                        SG[:, 512 * j:512 * j + 512], op0=ALU.mult, op1=ALU.mult),
                       waits=[("pe", pe_u), ("ac", ac_silu)], inc="dv")
                    up_free[j] = ("dv", cnt["dv"])
                    silu_free[j] = ("dv", cnt["dv"])
            barrier()
            if t == 0:
                dump("dh1", A[:])
                dump("dh2", Cao[:])

            SETS = ((0, 1), (2, 3), (6, 7))
            set_free = [("ac", 0)] * 3
            obs_free = [0, 0]
            d_evt = []
            dsg_free = [[("ad", 0)], [("ad", 0)]]
            for dg in range(8):
                sA, sB = SETS[(2 * dg) % 3], SETS[(2 * dg + 1) % 3]
                bkof = [sA[0], sA[1], sB[0], sB[1]]
                for ff in range(FFCH):
                    bi = 128 * (FFCH * dg + ff)
                    sl, r = slab(db[bi:bi + 128, :], 512, "vector", is_int=True)
                    src = (A, 512 * ff) if ff < 64 else (Cao, 512 * (ff - 64))
                    for j in range(4):
                        op("tensor", lambda e, ff=ff, j=j, sl=sl, src=src, bk=bkof[j]:
                            e.matmul(pb[bk][:, :], STB[sl][:, 128 * j:128 * j + 128],
                                     src[0][:, src[1]:src[1] + 512],
                                     start=(ff == 0), stop=(ff == FFCH - 1)),
                           waits=[r, set_free[(2 * dg + j // 2) % 3]],
                           inc="pe" if j == 3 else None)
                    slab_done(sl)
                pe_d = cnt["pe"]
                for j in range(4):
                    o = 4 * dg + j
                    sgs = o % 2
                    op("scalar", lambda e, j=j, o=o, sgs=sgs, bk=bkof[j]: e.activation(
                        SG32[:, 512 * sgs:512 * sgs + 512], pb[bk][:, :], AF.Copy,
                        scale=DSC[:, o:o + 1]),
                       waits=[("pe", pe_d)] + dsg_free[sgs], inc="ac")
                    ac_e = cnt["ac"]
                    if j == 1:
                        set_free[(2 * dg) % 3] = ("ac", ac_e)
                    if j == 3:
                        set_free[(2 * dg + 1) % 3] = ("ac", ac_e)
                    op("vector", lambda e, sgs=sgs, o=o: e.tensor_copy(
                        OBS[o % 2][:, :], SG32[:, 512 * sgs:512 * sgs + 512]),
                       waits=[("ac", ac_e), ("pe", obs_free[o % 2])])
                    op("vector", lambda e, sgs=sgs, o=o: e.tensor_tensor(
                        SQ[o % 2][:, 0:512], SG32[:, 512 * sgs:512 * sgs + 512],
                        SG32[:, 512 * sgs:512 * sgs + 512], op=ALU.mult),
                       inc="dv")
                    dv_e = cnt["dv"]
                    op("tensor", lambda e, o=o: e.matmul(pb[4][0:1, :], ONCB[:],
                                                         OBS[o % 2][:, :],
                                                         start=(o == 0), stop=(o == NCH - 1)),
                       waits=[("dv", dv_e)])
                    op("tensor", lambda e, o=o: e.matmul(pb[5][0:1, :], ONCB[:],
                                                         SQ[o % 2][:, 0:512],
                                                         start=(o == 0), stop=(o == NCH - 1)),
                       inc="pe")
                    obs_free[o % 2] = cnt["pe"]
                    op("scalar", lambda e, o=o, t=t, sgs=sgs: e.dma_start(
                        dsp[128 * (NCH * (t % 2) + o):
                            128 * (NCH * (t % 2) + o) + 128, :],
                        SG32[:, 512 * sgs:512 * sgs + 512]),
                       waits=[("ac", ac_e)], inc="sg%d" % sgs, inc_by=16)
                    d_evt.append(("sg%d" % sgs, cnt["sg%d" % sgs]))
                    dsg_free[sgs] = [("sg%d" % sgs, cnt["sg%d" % sgs]),
                                     ("dv", dv_e)]
            d_all = [("sg0", cnt["sg0"]), ("sg1", cnt["sg1"])]
            wbst["pe_p10"] = cnt["pe"]
            pe_bc3 = ln_smalls(512, [pb[4]], [pb[5]], [pb[6]], [pb[7]],
                               bank_free=[set_free[2]])

            for o in range(NCH):
                sl_d, gv_d = gdma(dsp[128 * (NCH * (t % 2) + o):
                                      128 * (NCH * (t % 2) + o) + 128, :],
                                  extra=d_all)
                sl_y, gv_y = gdma(y1sp[128 * (NCH * (t % 2) + o):
                                       128 * (NCH * (t % 2) + o) + 128, :])
                sgs = o % 2
                op("vector", lambda e, sl_d=sl_d: e.tensor_tensor(
                    T12[:, :], XG[sl_d][:, :], pb[6][:, :], op=ALU.subtract),
                   waits=[("xg%d" % sl_d, gv_d), ("xg%d" % sl_y, gv_y), ("pe", pe_bc3)] + dsg_free[sgs]
                         + out_free[sgs])
                op("vector", lambda e: e.tensor_tensor(
                    T12[:, :], T12[:, :], pb[7][:, :], op=ALU.mult))
                op("vector", lambda e, sl_y=sl_y, sgs=sgs: e.tensor_tensor(
                    SG32[:, 512 * sgs:512 * sgs + 512], T12[:, :], XG[sl_y][:, :],
                    op=ALU.add),
                   inc="dv")
                dv_o = cnt["dv"]
                xg["free"][sl_d] = ("dv", dv_o)
                xg["free"][sl_y] = ("dv", dv_o)
                op("scalar", lambda e, o=o, t=t, sgs=sgs: e.dma_start(
                    outT[128 * (NCH * t + o):128 * (NCH * t + o) + 128, :],
                    SG32[:, 512 * sgs:512 * sgs + 512]),
                   waits=[("dv", dv_o)], inc="os%d" % sgs, inc_by=16)
                out_free[sgs] = [("os%d" % sgs, cnt["os%d" % sgs])]

        op("gpsimd", lambda e: e.engine_nop(),
           waits=[("os0", cnt["os0"]), ("os1", cnt["os1"])])

        with nc.Block() as block:
            @block.sync
            def _(e):
                for fn in q["sync"]:
                    fn(e)

            @block.tensor
            def _(e):
                for fn in q["tensor"]:
                    fn(e)

            @block.scalar
            def _(e):
                for fn in q["scalar"]:
                    fn(e)

            @block.vector
            def _(e):
                for fn in q["vector"]:
                    fn(e)

            @block.gpsimd
            def _(e):
                for fn in q["gpsimd"]:
                    fn(e)

    return nc


def _rope_tables():
    inv = 1.0 / (10000.0 ** (np.arange(0, DH, 2, dtype=np.float32) / DH))
    tt = np.arange(S, dtype=np.float32)
    f = np.outer(tt, inv)
    cos = np.zeros((S, DH), dtype=np.float32)
    sin = np.zeros((S, DH), dtype=np.float32)
    cos[:, 0::2] = np.cos(f)
    cos[:, 1::2] = np.cos(f)
    sin[:, 0::2] = np.sin(f)
    sin[:, 1::2] = np.sin(f)
    return cos, sin


_NC_CACHE = {}


def make_inmaps(L0_attn_x, L0_attn_Wq, L0_attn_Wk, L0_attn_Wv, L0_attn_Wo,
                L0_ffn_gate_w, L0_ffn_gate_s, L0_ffn_up_w, L0_ffn_up_s,
                L0_ffn_down_w, L0_ffn_down_s):
    x = np.asarray(L0_attn_x, dtype=np.float32)
    wqT = np.ascontiguousarray(np.asarray(L0_attn_Wq, np.float32).T)
    wkT = np.ascontiguousarray(np.asarray(L0_attn_Wk, np.float32).T)
    wvT = np.ascontiguousarray(np.asarray(L0_attn_Wv, np.float32).T)
    woT = np.ascontiguousarray(np.asarray(L0_attn_Wo, np.float32).T)
    gT = np.ascontiguousarray(np.asarray(L0_ffn_gate_w, np.int32).T)
    uT = np.ascontiguousarray(np.asarray(L0_ffn_up_w, np.int32).T)
    dT = np.ascontiguousarray(np.asarray(L0_ffn_down_w, np.int32).T)
    gsc = np.ascontiguousarray(np.asarray(L0_ffn_gate_s, np.float32).reshape(FFCH, 128).T)
    usc = np.ascontiguousarray(np.asarray(L0_ffn_up_s, np.float32).reshape(FFCH, 128).T)
    dsc = np.ascontiguousarray(np.asarray(L0_ffn_down_s, np.float32).reshape(NCH, 128).T)

    def _blk_og(w):
        return np.ascontiguousarray(
            w.reshape(NCH, 128, 8, 512).transpose(2, 0, 1, 3).reshape(-1, 512))
    wqb = _blk_og(wqT)
    wob = _blk_og(woT)
    gpad = np.zeros((D, 22 * 512), np.int32)
    gpad[:, :DFF] = gT
    gbb = np.ascontiguousarray(
        gpad.reshape(NCH, 128, 22, 512).transpose(2, 0, 1, 3).reshape(-1, 512))
    gpad[:, :DFF] = uT
    ubb = np.ascontiguousarray(
        gpad.reshape(NCH, 128, 22, 512).transpose(2, 0, 1, 3).reshape(-1, 512))
    dbb = np.ascontiguousarray(
        dT.reshape(FFCH, 128, 8, 512).transpose(2, 0, 1, 3).reshape(-1, 512))

    cos, sin = _rope_tables()
    sgn = np.concatenate([-np.ones(64, np.float32), np.ones(64, np.float32)])
    ramp = np.linspace(0.0, 1.0, 256, dtype=np.float32)
    normal = np.concatenate([1.0 - ramp, ramp]).astype(np.float32)
    first = np.concatenate([np.zeros(256, np.float32), np.ones(256, np.float32)])
    onr = np.ones((1, 128), np.float32)
    onc = np.ones((128, 1), np.float32)

    xflat = x.reshape(B * S, D)
    in_maps = []
    for c in range(NCORES):
        seq, chunk = divmod(c, 4)
        lo = chunk * OWN - 256
        hi = chunk * OWN + OWN + 256
        seg = np.zeros((HALO, D), np.float32)
        s0, s1 = max(lo, 0), min(hi, S)
        seg[s0 - lo:s1 - lo] = xflat[seq * S + s0:seq * S + s1]
        segT = seg.T
        xp1 = np.ascontiguousarray(np.stack(
            [segT[:, 512 * t:512 * t + 1024].reshape(NCH, 128, 1024)
             for t in range(NT)]).reshape(-1, 1024))
        xp7 = np.ascontiguousarray(np.stack(
            [segT[:, 512 * t + 256:512 * t + 768].reshape(NCH, 128, 512)
             for t in range(NT)]).reshape(-1, 512))
        pos_q = chunk * OWN + np.arange(OWN)
        pos_k = np.clip(lo + np.arange(HALO), 0, S - 1)
        alpha = np.empty((1, 1024), np.float32)
        alpha[0, 0:512] = first if chunk == 0 else normal
        alpha[0, 512:1024] = normal
        mask = (np.zeros if chunk == 3 else np.ones)((128, 512), ml_dtypes.bfloat16)
        in_maps.append(dict(
            xp1=xp1, xp7=xp7,
            wqb=wqb, wkT=wkT, wvT=wvT, wob=wob, gb=gbb, ub=ubb, db=dbb,
            gsc=gsc, usc=usc, dsc=dsc,
            qcosT=np.ascontiguousarray(cos[pos_q].T).astype(ml_dtypes.bfloat16),
            qsinT=np.ascontiguousarray((sin[pos_q] * sgn).T).astype(ml_dtypes.bfloat16),
            kcosT=np.ascontiguousarray(cos[pos_k].T).astype(ml_dtypes.bfloat16),
            ksinT=np.ascontiguousarray((sin[pos_k] * sgn).T).astype(ml_dtypes.bfloat16),
            alpha=alpha, mask=mask, onesr=onr, onesc=onc,
        ))
    return in_maps


def kernel(**inputs):
    in_maps = make_inmaps(**inputs)
    if "nc" not in _NC_CACHE:
        _NC_CACHE["nc"] = build_nc()
    nc = _NC_CACHE["nc"]
    res = run_bass_kernel_spmd(nc, in_maps, core_ids=list(range(NCORES)))
    rows = []
    for r in res.results:
        blk = r["outT"].reshape(NT, NCH, 128, 512)
        full = blk.transpose(1, 2, 0, 3).reshape(D, OWN)
        rows.append(full.T)
    out = np.concatenate(rows, axis=0).reshape(B, S, D)
    return np.ascontiguousarray(out.astype(np.float32))


# version 16 (final)
# speedup vs baseline: 1.3431x; 1.2200x over previous
import numpy as np
import ml_dtypes
import concourse.bass as bass
import concourse.mybir as mybir
from concourse.bass_utils import run_bass_kernel_spmd
from contextlib import ExitStack

F32 = mybir.dt.float32
BF16 = mybir.dt.bfloat16
I32 = mybir.dt.int32
AF = mybir.ActivationFunctionType
ALU = mybir.AluOpType
f32 = np.float32

B, S, D = 2, 8192, 4096
H, DH = 32, 128
DFF = 11008
NCORES = 8
OWN = 2048
HALO = OWN + 512
T = 512
NT = OWN // T
NCH = D // 128
FFCH = DFF // 128
ATT_SCALE = 1.0 / float(np.sqrt(DH))
EPS = 1e-5
NSLAB = 6
NXG = 4

ENGINES = ("sync", "tensor", "scalar", "vector", "gpsimd")
SEMS = (["ws%d" % i for i in range(NSLAB)]
        + ["xg%d" % i for i in range(NXG)]
        + ["xs0", "xs1", "sg0", "sg1",
           "os0", "os1", "ad", "od", "pe", "ac", "dv", "gp"])


def build_nc(ntiles=NT, debug=False, lockstep=False):
    nc = bass.Bass("TRN2", target_bir_lowering=False)

    xp1 = nc.dram_tensor("xp1", [NT * NCH * 128, 1024], F32,
                         kind="ExternalInput").ap()
    xp7 = nc.dram_tensor("xp7", [NT * NCH * 128, 512], F32,
                         kind="ExternalInput").ap()
    wqb = nc.dram_tensor("wqb", [8 * NCH * 128, 512], F32,
                         kind="ExternalInput").ap()
    wob = nc.dram_tensor("wob", [8 * NCH * 128, 512], F32,
                         kind="ExternalInput").ap()
    wkT = nc.dram_tensor("wkT", [D, DH], F32, kind="ExternalInput").ap()
    wvT = nc.dram_tensor("wvT", [D, DH], F32, kind="ExternalInput").ap()
    gb = nc.dram_tensor("gb", [22 * NCH * 128, 512], I32,
                        kind="ExternalInput").ap()
    ub = nc.dram_tensor("ub", [22 * NCH * 128, 512], I32,
                        kind="ExternalInput").ap()
    db = nc.dram_tensor("db", [8 * FFCH * 128, 512], I32,
                        kind="ExternalInput").ap()
    gsc_d = nc.dram_tensor("gsc", [128, FFCH], F32, kind="ExternalInput").ap()
    usc_d = nc.dram_tensor("usc", [128, FFCH], F32, kind="ExternalInput").ap()
    dsc_d = nc.dram_tensor("dsc", [128, NCH], F32, kind="ExternalInput").ap()
    qcos_d = nc.dram_tensor("qcosT", [DH, OWN], BF16, kind="ExternalInput").ap()
    qsin_d = nc.dram_tensor("qsinT", [DH, OWN], BF16, kind="ExternalInput").ap()
    kcos_d = nc.dram_tensor("kcosT", [DH, HALO], BF16, kind="ExternalInput").ap()
    ksin_d = nc.dram_tensor("ksinT", [DH, HALO], BF16, kind="ExternalInput").ap()
    alpha_d = nc.dram_tensor("alpha", [1, 1024], F32, kind="ExternalInput").ap()
    mask_d = nc.dram_tensor("mask", [128, 512], BF16, kind="ExternalInput").ap()
    onr_d = nc.dram_tensor("onesr", [1, 128], F32, kind="ExternalInput").ap()
    onc_d = nc.dram_tensor("onesc", [128, 1], F32, kind="ExternalInput").ap()
    outT = nc.dram_tensor("outT", [NT * NCH * 128, 512], F32,
                          kind="ExternalOutput").ap()
    spk = "ExternalOutput" if debug else "Internal"
    y1sp = nc.dram_tensor("y1sp", [2 * NCH * 128, 512], F32, kind=spk).ap()
    dsp = nc.dram_tensor("dsp", [2 * NCH * 128, 512], F32, kind=spk).ap()
    dbg = {}
    if debug:
        for nm, sh, dt_ in (("dln1", [128, NCH * 1024], BF16),
                            ("dkr", [128, 1024], BF16), ("dvt", [128, 1024], BF16),
                            ("dqr", [128, NCH * 512], BF16),
                            ("dao", [128, NCH * 512], BF16),
                            ("dO", [128, NCH * 512], F32),
                            ("dln2", [128, NCH * 512], BF16),
                            ("dh1", [128, NCH * 1024], BF16),
                            ("dh2", [128, NCH * 512], BF16)):
            dbg[nm] = nc.dram_tensor(nm, sh, dt_, kind="ExternalOutput").ap()

    es = ExitStack()
    sb = lambda n, sh, dt: es.enter_context(nc.sbuf_tensor(n, sh, dt))
    with es:
        A = sb("A", [128, NCH * 1024], BF16)
        A32 = A.bitcast(F32)
        Bq = sb("Bq", [128, NCH * 512], BF16)
        Cao = sb("Cao", [128, NCH * 512], BF16)
        KR = sb("KR", [128, 1024], BF16)
        VT = sb("VT", [128, 1024], BF16)
        SS = sb("SS", [128, 2048], BF16)
        QCo = sb("QCo", [128, 512], BF16)
        QSi = sb("QSi", [128, 512], BF16)
        KCo = sb("KCo", [128, 1024], BF16)
        KSi = sb("KSi", [128, 1024], BF16)
        STG = [sb(f"STG{i}", [128, 512], F32) for i in range(NSLAB)]
        STGI = [t.bitcast(I32) for t in STG]
        STB = [sb(f"STB{i}", [128, 512], BF16) for i in range(NSLAB)]
        XST = [sb(f"XST{i}", [128, 1024], F32) for i in range(2)]
        SQ = [sb(f"SQ{i}", [128, 1024], BF16) for i in range(2)]
        OBS = [SQ[i][:, 512:1024] for i in range(2)]
        SG = sb("SG", [128, 2048], BF16)
        SG32 = SG.bitcast(F32)
        XG = [sb(f"XG{i}", [128, 512], F32) for i in range(NXG)]
        RPS = [sb("RPS0", [128, 512], F32)]
        T12 = sb("T12", [128, 512], F32)
        MR = sb("MR", [1, 1024], F32)
        RR = sb("RR", [1, 1024], F32)
        ARW = sb("ARW", [1, 1024], F32)
        ALP = sb("ALP", [1, 1024], F32)
        MSK = sb("MSK", [128, 512], BF16)
        GSC = sb("GSCs", [128, FFCH], F32)
        USC = sb("USCs", [128, FFCH], F32)
        DSC = sb("DSCs", [128, NCH], F32)
        ONR = sb("ONRs", [1, 128], F32)
        ONC = sb("ONCs", [128, 1], F32)
        ONCB = sb("ONCBs", [128, 1], BF16)
        pb = [es.enter_context(nc.psum_tensor(f"pb{i}", [128, 512], F32))
              for i in range(8)]
        sems = {s: es.enter_context(nc.semaphore(s)) for s in SEMS}

        cnt = {s: 0 for s in SEMS}
        hw = {e: {s: 0 for s in SEMS} for e in ENGINES}
        q = {e: [] for e in ENGINES}

        def op(eng, fn, waits=(), inc=None, inc_by=1):
            if lockstep:
                waits = list(waits) + [(s, cnt[s]) for s in SEMS]
            ws = []
            for s, v in waits:
                if v > 0 and v > hw[eng][s]:
                    hw[eng][s] = v
                    ws.append((sems[s], v))
            tgt = None
            if inc is not None:
                cnt[inc] += inc_by
                tgt = (sems[inc], inc_by)

            def emit(e, fn=fn, ws=ws, tgt=tgt):
                for h_, v in ws:
                    e.wait_ge(h_, v)
                r = fn(e)
                if tgt is not None:
                    r.then_inc(*tgt)
            q[eng].append(emit)

        def barrier():
            snap = [(s, cnt[s]) for s in ("pe", "ac", "dv", "ad", "gp")]
            for e in ("tensor", "scalar", "vector", "gpsimd"):
                for s, v in snap:
                    if v > hw[e][s]:
                        hw[e][s] = v

                        def emit(eng_, h_=sems[s], v_=v):
                            eng_.wait_ge(h_, v_)
                        q[e].append(emit)

        ring = {"seq": 0, "uses": [0] * NSLAB,
                "stage_free": [("ac", 0)] * NSLAB,
                "bf_free": [0] * NSLAB}

        def slab(src_ap, cols, caster, is_int=False):
            sl = ring["seq"] % NSLAB
            ring["seq"] += 1
            ring["uses"][sl] += 1
            wsem = "ws%d" % sl
            cs, cv = ring["stage_free"][sl]

            def dma_fn(e, sl=sl, c=cols, ii=is_int, s=src_ap):
                return e.dma_start(STGI[sl][:, 0:c] if ii else STG[sl][:, 0:c], s)
            op("sync", dma_fn, waits=[(cs, cv)], inc=wsem, inc_by=16)
            wd_v = cnt[wsem]
            csem = "dv" if caster == "vector" else "ac"

            if caster == "vector":
                def cast_fn(e, sl=sl, c=cols, ii=is_int):
                    return e.tensor_copy(STB[sl][:, 0:c],
                                         STGI[sl][:, 0:c] if ii else STG[sl][:, 0:c])
            else:
                def cast_fn(e, sl=sl, c=cols, ii=is_int):
                    return e.activation(STB[sl][:, 0:c],
                                        STGI[sl][:, 0:c] if ii else STG[sl][:, 0:c],
                                        AF.Copy, scale=f32(1.0))
            op(caster, cast_fn, waits=[(wsem, wd_v), ("pe", ring["bf_free"][sl])],
               inc=csem)
            ring["stage_free"][sl] = (csem, cnt[csem])
            return sl, (csem, cnt[csem])

        def slab_done(sl):
            ring["bf_free"][sl] = cnt["pe"]

        def dump(nm, src_ap):
            if not debug:
                return
            snap = [(s, cnt[s]) for s in ("pe", "ac", "dv", "ad", "gp")]
            op("gpsimd", lambda e, d=dbg[nm], s=src_ap: e.dma_start(d, s),
               waits=snap, inc="od", inc_by=16)
            odv = cnt["od"]
            for e in ("tensor", "scalar", "vector", "gpsimd"):
                hw[e]["od"] = odv

                def emit(eng_, h_=sems["od"], v_=odv):
                    eng_.wait_ge(h_, v_)
                q[e].append(emit)

        xst_free = [[("ac", 0)], [("ac", 0)]]

        def xdma(dst_slice, src_ap, slot, extra=()):
            ws = list(xst_free[slot]) + list(extra)
            op("scalar", lambda e, d=dst_slice, sa=src_ap: e.dma_start(d, sa),
               waits=ws, inc="xs%d" % slot, inc_by=16)
            return cnt["xs%d" % slot]

        xg = {"seq": 0, "free": [("dv", 0)] * NXG}

        def gdma(src_ap, extra=()):
            sl = xg["seq"] % NXG
            xg["seq"] += 1
            ws = [xg["free"][sl]] + list(extra)
            op("sync", lambda e, sl=sl, sa=src_ap: e.dma_start(XG[sl][:, :], sa),
               waits=ws, inc="xg%d" % sl, inc_by=16)
            return sl, cnt["xg%d" % sl]

        def ln_smalls(W, s_banks, q_banks, m_banks, r_banks, bank_free=()):
            nh = W // 512
            pe_stats = cnt["pe"]
            for h in range(nh):
                op("scalar", lambda e, h=h, b=s_banks[h]:
                    e.activation(MR[0:1, 512 * h:512 * h + 512], b[0:1, :], AF.Copy,
                                 scale=f32(1.0 / D)),
                   waits=[("pe", pe_stats)])
            for h in range(nh):
                op("scalar", lambda e, h=h:
                    e.activation(RR[0:1, 512 * h:512 * h + 512],
                                 MR[0:1, 512 * h:512 * h + 512],
                                 AF.Square, scale=f32(1.0)),
                   inc="ac" if h == nh - 1 else None)
            ac_m = cnt["ac"]
            for h in range(nh):
                op("tensor", lambda e, h=h, b=m_banks[h]:
                    e.matmul(b[:, :], ONR[0:1, :], MR[0:1, 512 * h:512 * h + 512],
                             start=True, stop=True),
                   waits=[("ac", ac_m)] + list(bank_free),
                   inc="pe" if h == nh - 1 else None)
            pe_mb = cnt["pe"]
            for h in range(nh):
                op("vector", lambda e, h=h, b=q_banks[h]:
                    e.scalar_tensor_tensor(MR[0:1, 512 * h:512 * h + 512], b[0:1, :],
                                           float(1.0 / D), RR[0:1, 512 * h:512 * h + 512],
                                           op0=ALU.mult, op1=ALU.subtract),
                   waits=[("pe", pe_mb), ("ac", ac_m)],
                   inc="dv" if h == nh - 1 else None)
            dv_var = cnt["dv"]
            for h in range(nh):
                op("scalar", lambda e, h=h:
                    e.activation(RR[0:1, 512 * h:512 * h + 512],
                                 MR[0:1, 512 * h:512 * h + 512],
                                 AF.Sqrt, bias=f32(EPS), scale=f32(1.0)),
                   waits=[("dv", dv_var)], inc="ac" if h == nh - 1 else None)
            ac_sq = cnt["ac"]
            op("vector", lambda e, W=W: e.reciprocal(MR[0:1, 0:W], RR[0:1, 0:W]),
               waits=[("ac", ac_sq)], inc="dv")
            dv_r = cnt["dv"]
            for h in range(nh):
                op("tensor", lambda e, h=h, b=r_banks[h]:
                    e.matmul(b[:, :], ONR[0:1, :], MR[0:1, 512 * h:512 * h + 512],
                             start=True, stop=True),
                   waits=[("dv", dv_r)], inc="pe" if h == nh - 1 else None)
            return cnt["pe"]

        for dst, src in ((GSC, gsc_d), (USC, usc_d), (DSC, dsc_d), (ALP, alpha_d),
                         (MSK, mask_d), (ONR, onr_d), (ONC, onc_d)):
            op("scalar", lambda e, d=dst, s=src: e.dma_start(d[:], s),
               inc="ad", inc_by=16)
        ad_init = cnt["ad"]
        op("vector", lambda e: e.tensor_copy(ONCB[:], ONC[:]), waits=[("ad", ad_init)],
           inc="dv")

        out_free = [[("os0", 0)], [("os1", 0)]]
        p5st = {"set_sig": [("ac", 0)] * 3, "set_rs": [("dv", 0)] * 3,
                "set_bc": [("dv", 0)] * 3, "ss_use": [0, 0],
                "av_free": [0, 0], "arw_free": [0, 0], "msk_free": 0,
                "qk_pe": {}, "sig_ac": {}, "sig_dv": {}, "rs_pe": {},
                "av_pe": {}, "arw_dv": {}}

        for t in range(ntiles):
            h0 = 512 * t
            rope_dv_prev = cnt["dv"]
            pe_tile0 = cnt["pe"]
            dv_tile0 = cnt["dv"]
            ac_tile0 = cnt["ac"]

            op("scalar", lambda e, t=t: e.dma_start(QCo[:], qcos_d[:, T * t:T * t + T]),
               waits=[("dv", rope_dv_prev)], inc="ad", inc_by=16)
            op("scalar", lambda e, t=t: e.dma_start(QSi[:], qsin_d[:, T * t:T * t + T]),
               inc="ad", inc_by=16)
            op("scalar", lambda e, h0=h0: e.dma_start(KCo[:], kcos_d[:, h0:h0 + 1024]),
               inc="ad", inc_by=16)
            op("scalar", lambda e, h0=h0: e.dma_start(KSi[:], ksin_d[:, h0:h0 + 1024]),
               inc="ad", inc_by=16)
            ad_rope = cnt["ad"]

            sq_free = [pe_tile0, pe_tile0]
            for i in range(NCH):
                ad_v = xdma(XST[i % 2][:],
                            xp1[128 * (NCH * t + i):128 * (NCH * t + i) + 128, :],
                            i % 2)
                op("scalar", lambda e, i=i: e.activation(A[:, 1024 * i:1024 * i + 1024],
                                                         XST[i % 2][:], AF.Copy,
                                                         scale=f32(1.0)),
                   waits=[("xs%d" % (i % 2), ad_v), ("pe", sq_free[i % 2]),
                          ("pe", pe_tile0)], inc="ac")
                ac_v = cnt["ac"]
                op("scalar", lambda e, i=i: e.activation(SQ[i % 2][:], XST[i % 2][:],
                                                         AF.Square, scale=f32(1.0)),
                   inc="gp")
                gp_v = cnt["gp"]
                xst_free[i % 2] = [("ac", ac_v), ("gp", gp_v)]
                for hh in range(2):
                    op("tensor", lambda e, i=i, hh=hh:
                        e.matmul(pb[hh][0:1, :], ONCB[:],
                                 A[:, 1024 * i + 512 * hh:1024 * i + 512 * hh + 512],
                                 start=(i == 0), stop=(i == NCH - 1)),
                       waits=[("ac", ac_v)])
                for hh in range(2):
                    op("tensor", lambda e, i=i, hh=hh:
                        e.matmul(pb[2 + hh][0:1, :], ONCB[:],
                                 SQ[i % 2][:, 512 * hh:512 * hh + 512],
                                 start=(i == 0), stop=(i == NCH - 1)),
                       waits=[("gp", gp_v)], inc="pe" if hh == 1 else None)
                sq_free[i % 2] = cnt["pe"]

            pe_bc1 = ln_smalls(1024, [pb[0], pb[1]], [pb[2], pb[3]],
                               [pb[4], pb[5]], [pb[6], pb[7]],
                               bank_free=[("dv", dv_tile0), ("ac", ac_tile0)])
            for i in range(NCH):
                for hh in range(2):
                    op("vector", lambda e, i=i, hh=hh:
                        e.tensor_tensor(XST[0][:, 512 * hh:512 * hh + 512],
                                        A[:, 1024 * i + 512 * hh:1024 * i + 512 * hh + 512],
                                        pb[4 + hh][:, :], op=ALU.subtract),
                       waits=[("pe", pe_bc1)])
                for hh in range(2):
                    op("vector", lambda e, i=i, hh=hh:
                        e.tensor_tensor(A[:, 1024 * i + 512 * hh:1024 * i + 512 * hh + 512],
                                        XST[0][:, 512 * hh:512 * hh + 512],
                                        pb[6 + hh][:, :], op=ALU.mult),
                       inc="dv" if hh == 1 else None)
            dv_ln1 = cnt["dv"]
            xst_free[0] = [("dv", dv_ln1)]
            barrier()
            if t == 0:
                dump("dln1", A[:])

            ac_v0 = 0
            for vpass in range(2):
                for i in range(NCH):
                    if vpass == 0:
                        slk, rk = slab(wkT[128 * i:128 * i + 128, :], 128, "scalar")
                    slv, rv = slab(wvT[128 * i:128 * i + 128, :], 128, "scalar")
                    if vpass == 0:
                        for hh in range(2):
                            op("tensor", lambda e, i=i, hh=hh, sl=slk:
                                e.matmul(pb[hh][:, :], STB[sl][:, 0:128],
                                         A[:, 1024 * i + 512 * hh:1024 * i + 512 * hh + 512],
                                         start=(i == 0), stop=(i == NCH - 1)),
                               waits=[rk, ("dv", dv_ln1)],
                               inc="pe" if hh == 1 else None)
                        slab_done(slk)
                    for kc in range(4):
                        ck = 4 * vpass + kc
                        op("tensor", lambda e, i=i, kc=kc, ck=ck, sl=slv:
                            e.matmul(pb[2 + kc][:, 0:128],
                                     A[:, 1024 * i + 128 * ck:1024 * i + 128 * ck + 128],
                                     STB[sl][:, 0:128],
                                     start=(i == 0), stop=(i == NCH - 1)),
                           waits=[rv, ("dv", dv_ln1), ("ac", ac_v0)],
                           inc="pe" if kc == 3 else None)
                    slab_done(slv)
                pe_v = cnt["pe"]
                for kc in range(4):
                    ck = 4 * vpass + kc
                    op("scalar", lambda e, kc=kc, ck=ck:
                        e.activation(VT[:, 128 * ck:128 * ck + 128], pb[2 + kc][:, 0:128],
                                     AF.Copy, scale=f32(1.0)),
                       waits=[("pe", pe_v)], inc="ac" if kc == 3 else None)
                ac_v0 = cnt["ac"]
                if vpass == 0:
                    pe_k = pe_v
                    for hh in range(2):
                        c0 = 512 * hh
                        op("scalar", lambda e, hh=hh, c0=c0: e.activation(
                            XST[1][0:64, c0:c0 + 512], pb[hh][64:128, :], AF.Copy,
                            scale=f32(1.0)),
                           waits=[("pe", pe_k)])
                        op("scalar", lambda e, hh=hh, c0=c0: e.activation(
                            XST[1][64:128, c0:c0 + 512], pb[hh][0:64, :], AF.Copy,
                            scale=f32(1.0)),
                           inc="ac")
                        ac_rot = cnt["ac"]
                        op("vector", lambda e, hh=hh, c0=c0: e.tensor_tensor(
                            XST[0][:, c0:c0 + 512], XST[1][:, c0:c0 + 512],
                            KSi[:, c0:c0 + 512], op=ALU.mult),
                           waits=[("ac", ac_rot), ("ad", ad_rope)])
                        op("vector", lambda e, hh=hh, c0=c0: e.tensor_tensor(
                            XST[1][:, c0:c0 + 512], pb[hh][:, :],
                            KCo[:, c0:c0 + 512], op=ALU.mult))
                        op("vector", lambda e, hh=hh, c0=c0: e.tensor_tensor(
                            KR[:, c0:c0 + 512], XST[0][:, c0:c0 + 512],
                            XST[1][:, c0:c0 + 512], op=ALU.add),
                           inc="dv")
                    xst_free[0] = [("dv", cnt["dv"])]
                    xst_free[1] = [("dv", cnt["dv"])]
            barrier()
            if t == 0:
                dump("dkr", KR[:])
                dump("dvt", VT[:])

            qrope_free = [0, 0]
            for og in range(8):
                pbase = (og % 2) * 4
                for i in range(NCH):
                    bi = 128 * (NCH * og + i)
                    sl, r = slab(wqb[bi:bi + 128, :], 512, "scalar")
                    for j in range(4):
                        op("tensor", lambda e, i=i, j=j, sl=sl, pbase=pbase:
                            e.matmul(pb[pbase + j][:, :], STB[sl][:, 128 * j:128 * j + 128],
                                     A[:, 1024 * i + 256:1024 * i + 768],
                                     start=(i == 0), stop=(i == NCH - 1)),
                           waits=[r, ("dv", qrope_free[og % 2])],
                           inc="pe" if j == 3 else None)
                    slab_done(sl)
                pe_q = cnt["pe"]
                for j in range(4):
                    hq = 4 * og + j
                    bk = pb[pbase + j]
                    op("scalar", lambda e, bk=bk: e.activation(
                        XST[0][0:64, 0:512], bk[64:128, :], AF.Copy, scale=f32(1.0)),
                       waits=[("pe", pe_q), ("dv", cnt["dv"])])
                    op("scalar", lambda e, bk=bk: e.activation(
                        XST[0][64:128, 0:512], bk[0:64, :], AF.Copy, scale=f32(1.0)),
                       inc="ac")
                    ac_rot = cnt["ac"]
                    op("vector", lambda e: e.tensor_tensor(
                        XST[0][:, 512:1024], XST[0][:, 0:512], QSi[:, :], op=ALU.mult),
                       waits=[("ac", ac_rot), ("ad", ad_rope)])
                    op("vector", lambda e, bk=bk: e.tensor_tensor(
                        XST[0][:, 0:512], bk[:, :], QCo[:, :], op=ALU.mult))
                    op("vector", lambda e, hq=hq: e.tensor_tensor(
                        Bq[:, 512 * hq:512 * hq + 512], XST[0][:, 512:1024],
                        XST[0][:, 0:512], op=ALU.add),
                       inc="dv")
                qrope_free[og % 2] = cnt["dv"]
            xst_free[0] = [("dv", cnt["dv"])]
            barrier()
            if t == 0:
                dump("dqr", Bq[:])

            SCB = ((0, 1), (2, 3), (4, 5))

            def p5_qk(s):
                bh, w = s // 2, s % 2
                b, hq = bh % 2, bh // 2
                qc0 = 512 * hq + 256 * b
                kb = 256 * b + 256 * w
                st = s % 3
                ws = [p5st["set_sig"][st], p5st["set_rs"][st], p5st["set_bc"][st]]
                for kc in range(4):
                    bank = pb[SCB[st][kc // 2]]
                    cc = 256 * (kc % 2)
                    op("tensor", lambda e, bank=bank, cc=cc, kb=kb, kc=kc, qc0=qc0:
                        e.matmul(bank[:, cc:cc + 256],
                                 KR[:, kb + 128 * kc:kb + 128 * kc + 128],
                                 Bq[:, qc0:qc0 + 256], start=True, stop=True),
                       waits=ws if kc == 0 else (),
                       inc="pe" if kc == 3 else None)
                p5st["qk_pe"][s] = cnt["pe"]

            def p5_sig(s):
                bh, w = s // 2, s % 2
                b = bh % 2
                st = s % 3
                masked = (t == NT - 1 and b == 1 and w == 1)
                base = 1024 * (s % 2)
                ws = [("pe", p5st["qk_pe"][s]), ("pe", p5st["ss_use"][s % 2])]
                op("scalar", lambda e, st=st, base=base:
                    e.activation(SS[:, base:base + 512], pb[SCB[st][0]][:, :],
                                 AF.Sigmoid, scale=f32(ATT_SCALE)),
                   waits=ws)
                if masked:
                    op("scalar", lambda e, st=st:
                        e.activation(SQ[0][:, 0:512], pb[SCB[st][1]][:, :],
                                     AF.Sigmoid, scale=f32(ATT_SCALE)),
                       waits=[("dv", p5st["msk_free"])], inc="ac")
                    op("vector", lambda e, base=base:
                        e.tensor_tensor(SS[:, base + 512:base + 1024],
                                        SQ[0][:, 0:512], MSK[:, :], op=ALU.mult),
                       waits=[("ac", cnt["ac"])], inc="dv")
                    p5st["msk_free"] = cnt["dv"]
                    p5st["sig_dv"][s] = cnt["dv"]
                else:
                    op("scalar", lambda e, st=st, base=base:
                        e.activation(SS[:, base + 512:base + 1024],
                                     pb[SCB[st][1]][:, :], AF.Sigmoid,
                                     scale=f32(ATT_SCALE)),
                       inc="ac")
                    p5st["sig_dv"][s] = 0
                p5st["sig_ac"][s] = cnt["ac"]
                p5st["set_sig"][st] = ("ac", cnt["ac"])

            def p5_rs_av(s):
                bh, w = s // 2, s % 2
                b = bh % 2
                kb = 256 * b + 256 * w
                st = s % 3
                base = 1024 * (s % 2)
                ws = [("ac", p5st["sig_ac"][s])]
                if p5st["sig_dv"][s]:
                    ws.append(("dv", p5st["sig_dv"][s]))
                rsb = pb[SCB[st][0]]
                for kc in range(4):
                    op("tensor", lambda e, rsb=rsb, base=base, kc=kc:
                        e.matmul(rsb[0:1, 0:256], ONCB[:],
                                 SS[:, base + 256 * kc:base + 256 * kc + 256],
                                 start=(kc == 0), stop=(kc == 3)),
                       waits=ws if kc == 0 else (),
                       inc="pe" if kc == 3 else None)
                p5st["rs_pe"][s] = cnt["pe"]
                avb = pb[6 + (s // 2) % 2]
                avc = 256 * (s % 2)
                avw = [("dv", p5st["av_free"][(s // 2) % 2])]
                for kc in range(4):
                    ck = kb // 128 + kc
                    op("tensor", lambda e, avb=avb, avc=avc, ck=ck, base=base, kc=kc:
                        e.matmul(avb[:, avc:avc + 256],
                                 VT[:, 128 * ck:128 * ck + 128],
                                 SS[:, base + 256 * kc:base + 256 * kc + 256],
                                 start=(kc == 0), stop=(kc == 3)),
                       waits=avw if kc == 0 else (),
                       inc="pe" if kc == 3 else None)
                p5st["av_pe"][s] = cnt["pe"]
                p5st["ss_use"][s % 2] = cnt["pe"]

            def p5_vec_post(s):
                bh, w = s // 2, s % 2
                b = bh % 2
                st = s % 3
                k = s // 2
                arwc = 512 * (k % 2) + 256 * w
                sel = (0 if (t == 0 and b == 0) else 512) + 256 * w
                rsb = pb[SCB[st][0]]
                rrc = 256 * (s % 4)
                op("vector", lambda e, rsb=rsb, rrc=rrc:
                    e.reciprocal(RR[0:1, rrc:rrc + 256], rsb[0:1, 0:256]),
                   waits=[("pe", p5st["rs_pe"][s])])
                op("vector", lambda e, arwc=arwc, sel=sel, rrc=rrc:
                    e.tensor_tensor(ARW[0:1, arwc:arwc + 256],
                                    RR[0:1, rrc:rrc + 256],
                                    ALP[0:1, sel:sel + 256], op=ALU.mult),
                   waits=[("pe", p5st["arw_free"][k % 2]), ("ad", ad_init)],
                   inc="dv")
                p5st["set_rs"][st] = ("dv", cnt["dv"])
                p5st["arw_dv"][s] = cnt["dv"]

            def p5_bc_blend(k):
                st = (2 * k + 2) % 3
                bcb = pb[SCB[st][1]]
                arw0 = 512 * (k % 2)
                op("tensor", lambda e, bcb=bcb, arw0=arw0:
                    e.matmul(bcb[:, :], ONR[0:1, :], ARW[0:1, arw0:arw0 + 512],
                             start=True, stop=True),
                   waits=[("dv", p5st["arw_dv"][2 * k + 1]),
                          p5st["set_sig"][st], p5st["set_bc"][st]],
                   inc="pe")
                p5st["arw_free"][k % 2] = cnt["pe"]
                bc_pe = cnt["pe"]
                op("vector", lambda e, bcb=bcb: e.tensor_copy(RPS[0][:, :], bcb[:, :]),
                   waits=[("pe", bc_pe)])
                avb = pb[6 + (k % 2)]
                op("vector", lambda e, avb=avb: e.tensor_tensor(
                    T12[:, 0:256], avb[:, 0:256], RPS[0][:, 0:256], op=ALU.mult),
                   waits=[("pe", p5st["av_pe"][2 * k])])
                op("vector", lambda e, avb=avb: e.tensor_tensor(
                    T12[:, 256:512], avb[:, 256:512], RPS[0][:, 256:512], op=ALU.mult),
                   waits=[("pe", p5st["av_pe"][2 * k + 1])])
                b, hq = k % 2, k // 2
                op("vector", lambda e, hq=hq, b=b: e.tensor_tensor(
                    Cao[:, 512 * hq + 256 * b:512 * hq + 256 * b + 256],
                    T12[:, 0:256], T12[:, 256:512], op=ALU.add),
                   inc="dv")
                p5st["av_free"][k % 2] = cnt["dv"]
                p5st["set_bc"][st] = ("dv", cnt["dv"])

            NSLOT = 128
            for s in range(NSLOT + 2):
                if s < NSLOT:
                    p5_qk(s)
                    p5_sig(s)
                if 1 <= s <= NSLOT:
                    p5_rs_av(s - 1)
                    p5_vec_post(s - 1)
                if s >= 3 and s % 2 == 1:
                    p5_bc_blend((s - 3) // 2)
            barrier()
            if t == 0:
                dump("dao", Cao[:])

            SETS = ((0, 1), (2, 3), (6, 7))
            set_free = [("ac", 0)] * 3
            obs_free = [0, 0]
            for og in range(8):
                sA, sB = SETS[(2 * og) % 3], SETS[(2 * og + 1) % 3]
                bkof = [sA[0], sA[1], sB[0], sB[1]]
                for i in range(NCH):
                    bi = 128 * (NCH * og + i)
                    sl, r = slab(wob[bi:bi + 128, :], 512, "scalar")
                    for j in range(4):
                        op("tensor", lambda e, i=i, j=j, sl=sl, bk=bkof[j]:
                            e.matmul(pb[bk][:, :], STB[sl][:, 128 * j:128 * j + 128],
                                     Cao[:, 512 * i:512 * i + 512],
                                     start=(i == 0), stop=(i == NCH - 1)),
                           waits=[r, set_free[(2 * og + j // 2) % 3]],
                           inc="pe" if j == 3 else None)
                    slab_done(sl)
                pe_wo = cnt["pe"]
                for j in range(4):
                    o = 4 * og + j
                    op("scalar", lambda e, j=j, o=o, bk=bkof[j]: e.activation(
                        A32[:, 512 * o:512 * o + 512], pb[bk][:, :], AF.Copy,
                        scale=f32(1.0)),
                       waits=[("pe", pe_wo)], inc="ac")
                    ac_e = cnt["ac"]
                    if j == 1:
                        set_free[(2 * og) % 3] = ("ac", ac_e)
                    if j == 3:
                        set_free[(2 * og + 1) % 3] = ("ac", ac_e)
                    op("vector", lambda e, o=o: e.tensor_copy(
                        OBS[o % 2][:, :], A32[:, 512 * o:512 * o + 512]),
                       waits=[("ac", ac_e), ("pe", obs_free[o % 2])])
                    op("vector", lambda e, o=o: e.tensor_tensor(
                        SQ[o % 2][:, 0:512], A32[:, 512 * o:512 * o + 512],
                        A32[:, 512 * o:512 * o + 512], op=ALU.mult),
                       inc="dv")
                    dv_e = cnt["dv"]
                    op("tensor", lambda e, o=o: e.matmul(pb[4][0:1, :], ONCB[:],
                                                         OBS[o % 2][:, :],
                                                         start=(o == 0), stop=(o == NCH - 1)),
                       waits=[("dv", dv_e)])
                    op("tensor", lambda e, o=o: e.matmul(pb[5][0:1, :], ONCB[:],
                                                         SQ[o % 2][:, 0:512],
                                                         start=(o == 0), stop=(o == NCH - 1)),
                       inc="pe")
                    obs_free[o % 2] = cnt["pe"]
            if t == 0:
                dump("dO", A32[:])
            pe_bcO = ln_smalls(512, [pb[4]], [pb[5]], [pb[6]], [pb[7]],
                               bank_free=[set_free[2]])

            sg_free = [list(out_free[0]), list(out_free[1])]
            y1_evt = []
            for o in range(NCH):
                sl, gv = gdma(xp7[128 * (NCH * t + o):128 * (NCH * t + o) + 128, :])
                sgs = o % 2
                op("vector", lambda e, o=o: e.tensor_tensor(
                    T12[:, :], A32[:, 512 * o:512 * o + 512], pb[6][:, :],
                    op=ALU.subtract),
                   waits=[("xg%d" % sl, gv), ("pe", pe_bcO)] + sg_free[sgs])
                op("vector", lambda e: e.tensor_tensor(
                    T12[:, :], T12[:, :], pb[7][:, :], op=ALU.mult))
                op("vector", lambda e, o=o, sgs=sgs, sl=sl: e.tensor_tensor(
                    SG32[:, 512 * sgs:512 * sgs + 512], T12[:, :], XG[sl][:, :],
                    op=ALU.add),
                   inc="dv")
                dv_y1 = cnt["dv"]
                xg["free"][sl] = ("dv", dv_y1)
                op("scalar", lambda e, sgs=sgs: e.activation(
                    OBS[sgs][:, :], SG32[:, 512 * sgs:512 * sgs + 512], AF.Copy,
                    scale=f32(1.0)),
                   waits=[("dv", dv_y1)])
                op("scalar", lambda e, sgs=sgs: e.activation(
                    SQ[sgs][:, 0:512], SG32[:, 512 * sgs:512 * sgs + 512], AF.Square,
                    scale=f32(1.0)),
                   inc="ac")
                ac_y = cnt["ac"]
                op("tensor", lambda e, o=o, sgs=sgs: e.matmul(
                    pb[0][0:1, :], ONCB[:], OBS[sgs][:, :],
                    start=(o == 0), stop=(o == NCH - 1)),
                   waits=[("ac", ac_y)])
                op("tensor", lambda e, o=o, sgs=sgs: e.matmul(
                    pb[1][0:1, :], ONCB[:], SQ[sgs][:, 0:512],
                    start=(o == 0), stop=(o == NCH - 1)),
                   inc="pe")
                op("scalar", lambda e, o=o, t=t, sgs=sgs: e.dma_start(
                    y1sp[128 * (NCH * (t % 2) + o):128 * (NCH * (t % 2) + o) + 128, :],
                    SG32[:, 512 * sgs:512 * sgs + 512]),
                   waits=[("dv", dv_y1)], inc="sg%d" % sgs, inc_by=16)
                y1_evt.append(("sg%d" % sgs, cnt["sg%d" % sgs]))
                sg_free[sgs] = [("sg%d" % sgs, cnt["sg%d" % sgs]), ("ac", ac_y)]
            y1_all = [("sg0", cnt["sg0"]), ("sg1", cnt["sg1"])]
            pe_bc2 = ln_smalls(512, [pb[0]], [pb[1]], [pb[2]], [pb[3]])

            for i in range(NCH):
                sl, gv = gdma(y1sp[128 * (NCH * (t % 2) + i):
                                   128 * (NCH * (t % 2) + i) + 128, :],
                              extra=y1_all)
                op("vector", lambda e, sl=sl: e.tensor_tensor(
                    T12[:, :], XG[sl][:, :], pb[2][:, :], op=ALU.subtract),
                   waits=[("xg%d" % sl, gv), ("pe", pe_bc2)])
                op("vector", lambda e, i=i: e.tensor_tensor(
                    Bq[:, 512 * i:512 * i + 512], T12[:, :], pb[3][:, :], op=ALU.mult),
                   inc="dv")
                xg["free"][sl] = ("dv", cnt["dv"])
            barrier()
            if t == 0:
                dump("dln2", Bq[:])

            silu_free = [("dv", 0)] * 4
            up_free = [("dv", 0)] * 4
            gate_free = [("ac", 0)] * 4
            gpre = []
            upre = []
            for g in range(22):
                gw = min(512, DFF - 512 * g)
                nch_g = gw // 128
                for i in range(NCH):
                    if gpre:
                        sl, r = gpre.pop(0)
                    else:
                        bi = 128 * (NCH * g + i)
                        sl, r = slab(gb[bi:bi + 128, :], 512, "vector",
                                     is_int=True)
                    for j in range(nch_g):
                        op("tensor", lambda e, i=i, j=j, sl=sl:
                            e.matmul(pb[j][:, :], STB[sl][:, 128 * j:128 * j + 128],
                                     Bq[:, 512 * i:512 * i + 512],
                                     start=(i == 0), stop=(i == NCH - 1)),
                           waits=[r, gate_free[j]],
                           inc="pe" if j == nch_g - 1 else None)
                    slab_done(sl)
                pe_g = cnt["pe"]
                upre = [slab(ub[128 * (NCH * g + i2):128 * (NCH * g + i2) + 128, :],
                             512, "scalar", is_int=True) for i2 in range(4)]
                for j in range(nch_g):
                    c = 4 * g + j
                    op("scalar", lambda e, j=j, c=c: e.activation(
                        SG[:, 512 * j:512 * j + 512], pb[j][:, :], AF.Silu,
                        scale=GSC[:, c:c + 1]),
                       waits=[("pe", pe_g), silu_free[j]], inc="ac")
                    gate_free[j] = ("ac", cnt["ac"])
                ac_silu = cnt["ac"]
                for i in range(NCH):
                    if upre:
                        sl, r = upre.pop(0)
                    else:
                        bi = 128 * (NCH * g + i)
                        sl, r = slab(ub[bi:bi + 128, :], 512, "scalar",
                                     is_int=True)
                    for j in range(nch_g):
                        op("tensor", lambda e, i=i, j=j, sl=sl:
                            e.matmul(pb[4 + j][:, :], STB[sl][:, 128 * j:128 * j + 128],
                                     Bq[:, 512 * i:512 * i + 512],
                                     start=(i == 0), stop=(i == NCH - 1)),
                           waits=[r, up_free[j]],
                           inc="pe" if j == nch_g - 1 else None)
                    slab_done(sl)
                pe_u = cnt["pe"]
                if g + 1 < 22:
                    gpre = [slab(gb[128 * (NCH * (g + 1) + i2):
                                    128 * (NCH * (g + 1) + i2) + 128, :],
                                 512, "vector", is_int=True) for i2 in range(4)]
                for j in range(nch_g):
                    c = 4 * g + j
                    dst = (A, 512 * c) if c < 64 else (Cao, 512 * (c - 64))
                    op("vector", lambda e, j=j, c=c, dst=dst: e.scalar_tensor_tensor(
                        dst[0][:, dst[1]:dst[1] + 512], pb[4 + j][:, :], USC[:, c:c + 1],
                        SG[:, 512 * j:512 * j + 512], op0=ALU.mult, op1=ALU.mult),
                       waits=[("pe", pe_u), ("ac", ac_silu)], inc="dv")
                    up_free[j] = ("dv", cnt["dv"])
                    silu_free[j] = ("dv", cnt["dv"])
            barrier()
            if t == 0:
                dump("dh1", A[:])
                dump("dh2", Cao[:])

            SETS = ((0, 1), (2, 3), (6, 7))
            set_free = [("ac", 0)] * 3
            obs_free = [0, 0]
            d_evt = []
            dsg_free = [[("ad", 0)], [("ad", 0)]]
            for dg in range(8):
                sA, sB = SETS[(2 * dg) % 3], SETS[(2 * dg + 1) % 3]
                bkof = [sA[0], sA[1], sB[0], sB[1]]
                for ff in range(FFCH):
                    bi = 128 * (FFCH * dg + ff)
                    sl, r = slab(db[bi:bi + 128, :], 512, "vector", is_int=True)
                    src = (A, 512 * ff) if ff < 64 else (Cao, 512 * (ff - 64))
                    for j in range(4):
                        op("tensor", lambda e, ff=ff, j=j, sl=sl, src=src, bk=bkof[j]:
                            e.matmul(pb[bk][:, :], STB[sl][:, 128 * j:128 * j + 128],
                                     src[0][:, src[1]:src[1] + 512],
                                     start=(ff == 0), stop=(ff == FFCH - 1)),
                           waits=[r, set_free[(2 * dg + j // 2) % 3]],
                           inc="pe" if j == 3 else None)
                    slab_done(sl)
                pe_d = cnt["pe"]
                for j in range(4):
                    o = 4 * dg + j
                    sgs = o % 2
                    op("scalar", lambda e, j=j, o=o, sgs=sgs, bk=bkof[j]: e.activation(
                        SG32[:, 512 * sgs:512 * sgs + 512], pb[bk][:, :], AF.Copy,
                        scale=DSC[:, o:o + 1]),
                       waits=[("pe", pe_d)] + dsg_free[sgs], inc="ac")
                    ac_e = cnt["ac"]
                    if j == 1:
                        set_free[(2 * dg) % 3] = ("ac", ac_e)
                    if j == 3:
                        set_free[(2 * dg + 1) % 3] = ("ac", ac_e)
                    op("vector", lambda e, sgs=sgs, o=o: e.tensor_copy(
                        OBS[o % 2][:, :], SG32[:, 512 * sgs:512 * sgs + 512]),
                       waits=[("ac", ac_e), ("pe", obs_free[o % 2])])
                    op("vector", lambda e, sgs=sgs, o=o: e.tensor_tensor(
                        SQ[o % 2][:, 0:512], SG32[:, 512 * sgs:512 * sgs + 512],
                        SG32[:, 512 * sgs:512 * sgs + 512], op=ALU.mult),
                       inc="dv")
                    dv_e = cnt["dv"]
                    op("tensor", lambda e, o=o: e.matmul(pb[4][0:1, :], ONCB[:],
                                                         OBS[o % 2][:, :],
                                                         start=(o == 0), stop=(o == NCH - 1)),
                       waits=[("dv", dv_e)])
                    op("tensor", lambda e, o=o: e.matmul(pb[5][0:1, :], ONCB[:],
                                                         SQ[o % 2][:, 0:512],
                                                         start=(o == 0), stop=(o == NCH - 1)),
                       inc="pe")
                    obs_free[o % 2] = cnt["pe"]
                    op("scalar", lambda e, o=o, t=t, sgs=sgs: e.dma_start(
                        dsp[128 * (NCH * (t % 2) + o):
                            128 * (NCH * (t % 2) + o) + 128, :],
                        SG32[:, 512 * sgs:512 * sgs + 512]),
                       waits=[("ac", ac_e)], inc="sg%d" % sgs, inc_by=16)
                    d_evt.append(("sg%d" % sgs, cnt["sg%d" % sgs]))
                    dsg_free[sgs] = [("sg%d" % sgs, cnt["sg%d" % sgs]),
                                     ("dv", dv_e)]
            d_all = [("sg0", cnt["sg0"]), ("sg1", cnt["sg1"])]
            pe_bc3 = ln_smalls(512, [pb[4]], [pb[5]], [pb[6]], [pb[7]],
                               bank_free=[set_free[2]])

            for o in range(NCH):
                sl_d, gv_d = gdma(dsp[128 * (NCH * (t % 2) + o):
                                      128 * (NCH * (t % 2) + o) + 128, :],
                                  extra=d_all)
                sl_y, gv_y = gdma(y1sp[128 * (NCH * (t % 2) + o):
                                       128 * (NCH * (t % 2) + o) + 128, :])
                sgs = o % 2
                op("vector", lambda e, sl_d=sl_d: e.tensor_tensor(
                    T12[:, :], XG[sl_d][:, :], pb[6][:, :], op=ALU.subtract),
                   waits=[("xg%d" % sl_d, gv_d), ("xg%d" % sl_y, gv_y), ("pe", pe_bc3)] + dsg_free[sgs]
                         + out_free[sgs])
                op("vector", lambda e: e.tensor_tensor(
                    T12[:, :], T12[:, :], pb[7][:, :], op=ALU.mult))
                op("vector", lambda e, sl_y=sl_y, sgs=sgs: e.tensor_tensor(
                    SG32[:, 512 * sgs:512 * sgs + 512], T12[:, :], XG[sl_y][:, :],
                    op=ALU.add),
                   inc="dv")
                dv_o = cnt["dv"]
                xg["free"][sl_d] = ("dv", dv_o)
                xg["free"][sl_y] = ("dv", dv_o)
                op("scalar", lambda e, o=o, t=t, sgs=sgs: e.dma_start(
                    outT[128 * (NCH * t + o):128 * (NCH * t + o) + 128, :],
                    SG32[:, 512 * sgs:512 * sgs + 512]),
                   waits=[("dv", dv_o)], inc="os%d" % sgs, inc_by=16)
                out_free[sgs] = [("os%d" % sgs, cnt["os%d" % sgs])]

        op("gpsimd", lambda e: e.engine_nop(),
           waits=[("os0", cnt["os0"]), ("os1", cnt["os1"])])

        with nc.Block() as block:
            @block.sync
            def _(e):
                for fn in q["sync"]:
                    fn(e)

            @block.tensor
            def _(e):
                for fn in q["tensor"]:
                    fn(e)

            @block.scalar
            def _(e):
                for fn in q["scalar"]:
                    fn(e)

            @block.vector
            def _(e):
                for fn in q["vector"]:
                    fn(e)

            @block.gpsimd
            def _(e):
                for fn in q["gpsimd"]:
                    fn(e)

    return nc


def _rope_tables():
    inv = 1.0 / (10000.0 ** (np.arange(0, DH, 2, dtype=np.float32) / DH))
    tt = np.arange(S, dtype=np.float32)
    f = np.outer(tt, inv)
    cos = np.zeros((S, DH), dtype=np.float32)
    sin = np.zeros((S, DH), dtype=np.float32)
    cos[:, 0::2] = np.cos(f)
    cos[:, 1::2] = np.cos(f)
    sin[:, 0::2] = np.sin(f)
    sin[:, 1::2] = np.sin(f)
    return cos, sin


_NC_CACHE = {}


def make_inmaps(L0_attn_x, L0_attn_Wq, L0_attn_Wk, L0_attn_Wv, L0_attn_Wo,
                L0_ffn_gate_w, L0_ffn_gate_s, L0_ffn_up_w, L0_ffn_up_s,
                L0_ffn_down_w, L0_ffn_down_s):
    x = np.asarray(L0_attn_x, dtype=np.float32)
    wqT = np.ascontiguousarray(np.asarray(L0_attn_Wq, np.float32).T)
    wkT = np.ascontiguousarray(np.asarray(L0_attn_Wk, np.float32).T)
    wvT = np.ascontiguousarray(np.asarray(L0_attn_Wv, np.float32).T)
    woT = np.ascontiguousarray(np.asarray(L0_attn_Wo, np.float32).T)
    gT = np.ascontiguousarray(np.asarray(L0_ffn_gate_w, np.int32).T)
    uT = np.ascontiguousarray(np.asarray(L0_ffn_up_w, np.int32).T)
    dT = np.ascontiguousarray(np.asarray(L0_ffn_down_w, np.int32).T)
    gsc = np.ascontiguousarray(np.asarray(L0_ffn_gate_s, np.float32).reshape(FFCH, 128).T)
    usc = np.ascontiguousarray(np.asarray(L0_ffn_up_s, np.float32).reshape(FFCH, 128).T)
    dsc = np.ascontiguousarray(np.asarray(L0_ffn_down_s, np.float32).reshape(NCH, 128).T)

    def _blk_og(w):
        return np.ascontiguousarray(
            w.reshape(NCH, 128, 8, 512).transpose(2, 0, 1, 3).reshape(-1, 512))
    wqb = _blk_og(wqT)
    wob = _blk_og(woT)
    gpad = np.zeros((D, 22 * 512), np.int32)
    gpad[:, :DFF] = gT
    gbb = np.ascontiguousarray(
        gpad.reshape(NCH, 128, 22, 512).transpose(2, 0, 1, 3).reshape(-1, 512))
    gpad[:, :DFF] = uT
    ubb = np.ascontiguousarray(
        gpad.reshape(NCH, 128, 22, 512).transpose(2, 0, 1, 3).reshape(-1, 512))
    dbb = np.ascontiguousarray(
        dT.reshape(FFCH, 128, 8, 512).transpose(2, 0, 1, 3).reshape(-1, 512))

    cos, sin = _rope_tables()
    sgn = np.concatenate([-np.ones(64, np.float32), np.ones(64, np.float32)])
    ramp = np.linspace(0.0, 1.0, 256, dtype=np.float32)
    normal = np.concatenate([1.0 - ramp, ramp]).astype(np.float32)
    first = np.concatenate([np.zeros(256, np.float32), np.ones(256, np.float32)])
    onr = np.ones((1, 128), np.float32)
    onc = np.ones((128, 1), np.float32)

    xflat = x.reshape(B * S, D)
    in_maps = []
    for c in range(NCORES):
        seq, chunk = divmod(c, 4)
        lo = chunk * OWN - 256
        hi = chunk * OWN + OWN + 256
        seg = np.zeros((HALO, D), np.float32)
        s0, s1 = max(lo, 0), min(hi, S)
        seg[s0 - lo:s1 - lo] = xflat[seq * S + s0:seq * S + s1]
        segT = seg.T
        xp1 = np.ascontiguousarray(np.stack(
            [segT[:, 512 * t:512 * t + 1024].reshape(NCH, 128, 1024)
             for t in range(NT)]).reshape(-1, 1024))
        xp7 = np.ascontiguousarray(np.stack(
            [segT[:, 512 * t + 256:512 * t + 768].reshape(NCH, 128, 512)
             for t in range(NT)]).reshape(-1, 512))
        pos_q = chunk * OWN + np.arange(OWN)
        pos_k = np.clip(lo + np.arange(HALO), 0, S - 1)
        alpha = np.empty((1, 1024), np.float32)
        alpha[0, 0:512] = first if chunk == 0 else normal
        alpha[0, 512:1024] = normal
        mask = (np.zeros if chunk == 3 else np.ones)((128, 512), ml_dtypes.bfloat16)
        in_maps.append(dict(
            xp1=xp1, xp7=xp7,
            wqb=wqb, wkT=wkT, wvT=wvT, wob=wob, gb=gbb, ub=ubb, db=dbb,
            gsc=gsc, usc=usc, dsc=dsc,
            qcosT=np.ascontiguousarray(cos[pos_q].T).astype(ml_dtypes.bfloat16),
            qsinT=np.ascontiguousarray((sin[pos_q] * sgn).T).astype(ml_dtypes.bfloat16),
            kcosT=np.ascontiguousarray(cos[pos_k].T).astype(ml_dtypes.bfloat16),
            ksinT=np.ascontiguousarray((sin[pos_k] * sgn).T).astype(ml_dtypes.bfloat16),
            alpha=alpha, mask=mask, onesr=onr, onesc=onc,
        ))
    return in_maps


def kernel(**inputs):
    in_maps = make_inmaps(**inputs)
    if "nc" not in _NC_CACHE:
        _NC_CACHE["nc"] = build_nc()
    nc = _NC_CACHE["nc"]
    res = run_bass_kernel_spmd(nc, in_maps, core_ids=list(range(NCORES)))
    rows = []
    for r in res.results:
        blk = r["outT"].reshape(NT, NCH, 128, 512)
        full = blk.transpose(1, 2, 0, 3).reshape(D, OWN)
        rows.append(full.T)
    out = np.concatenate(rows, axis=0).reshape(B, S, D)
    return np.ascontiguousarray(out.astype(np.float32))
